# version 06
# baseline (speedup 1.0000x reference)
import math
import numpy as np
import ml_dtypes
import concourse.bass as bass
import concourse.mybir as mybir
from concourse.bass_utils import run_bass_kernel_spmd
from contextlib import ExitStack

F32 = mybir.dt.float32
I32 = mybir.dt.int32
BF16 = mybir.dt.bfloat16
AF = mybir.ActivationFunctionType
ALU = mybir.AluOpType
AX = mybir.AxisListType

H = 2048
D = 128
NH = 16
KVH = 4
I_FF = 8192
T = 1024
NT = 8
NET = 16
WIN = 1152
INV_SQRT_D = 1.0 / math.sqrt(D)
NEG = -1e9

ENGINES = ("sync", "vector", "scalar", "tensor", "gpsimd")


class Sem:
    __slots__ = ("h", "n")
    def __init__(self, h):
        self.h = h
        self.n = 0


class Prog:
    def __init__(self):
        self.steps = {e: [] for e in ENGINES}

    def on(self, eng, fn):
        self.steps[eng].append(fn)

    def wait(self, eng, sem, val=None):
        v = sem.n if val is None else val
        if v > 0:
            self.on(eng, lambda E, s=sem, vv=v: E.wait_ge(s.h, vv))

    def op(self, eng, f, sem=None, k=1):
        if sem is None:
            self.on(eng, f)
            return None
        sem.n += k
        self.on(eng, lambda E, f=f, s=sem, kk=k: f(E).then_inc(s.h, kk))
        return sem.n

    def dma(self, eng, dst, src, sem, **kw):
        return self.op(eng, lambda E, d=dst, s=src: E.dma_start(d, s, **kw),
                       sem, 16)

    def seq(self, eng, f, sem):
        self.wait(eng, sem, sem.n)
        return self.op(eng, f, sem)


def build_program(debug=False, pbv=2):
    nc = bass.Bass("TRN2", target_bir_lowering=False, detect_race_conditions=False)
    es = ExitStack()

    x_ext = nc.dram_tensor("x_ext", [2048, H], F32, kind="ExternalInput")
    wq_d = nc.dram_tensor("wq", [H, H], F32, kind="ExternalInput")
    wk_d = nc.dram_tensor("wk", [H, 512], F32, kind="ExternalInput")
    wv_d = nc.dram_tensor("wv", [H, 512], F32, kind="ExternalInput")
    wo_d = nc.dram_tensor("wo", [H, H], F32, kind="ExternalInput")
    wg_d = nc.dram_tensor("wg", [I_FF, H], F32, kind="ExternalInput")
    wu_d = nc.dram_tensor("wu", [I_FF, H], F32, kind="ExternalInput")
    wd_d = nc.dram_tensor("wd", [H, I_FF], F32, kind="ExternalInput")
    eye_d = nc.dram_tensor("eye_bf", [128, 128], BF16, kind="ExternalInput")
    mlo_d = nc.dram_tensor("mask_lo", [128, 128], F32, kind="ExternalInput")
    mhi_d = nc.dram_tensor("mask_hi", [128, 128], F32, kind="ExternalInput")
    kones_d = nc.dram_tensor("kones", [128, NET], BF16, kind="ExternalInput")
    ones1_d = nc.dram_tensor("ones1", [1, 128], BF16, kind="ExternalInput")
    out_d = nc.dram_tensor("out", [T, H], F32, kind="ExternalOutput")

    qwg_d = nc.dram_tensor("qwg", [I_FF, H], BF16)
    qwu_d = nc.dram_tensor("qwu", [I_FF, H], BF16)
    qwd_d = nc.dram_tensor("qwd", [H, I_FF], BF16)
    xres_d = nc.dram_tensor("xres", [T, H], F32,
                            kind="ExternalOutput" if debug else "Internal")
    dbg = {}
    if debug:
        dbg["xq0"] = nc.dram_tensor("dbg_xq0", [128, H], BF16, kind="ExternalOutput")
        dbg["qwg0"] = nc.dram_tensor("dbg_qwg0", [128, H], BF16, kind="ExternalOutput")
        dbg["xlt0"] = nc.dram_tensor("dbg_xlt0", [128, 1024], BF16, kind="ExternalOutput")
        dbg["kfm0"] = nc.dram_tensor("dbg_kfm0", [128, 2048], BF16, kind="ExternalOutput")
        dbg["qtm0"] = nc.dram_tensor("dbg_qtm0", [128, 2048], BF16, kind="ExternalOutput")
        dbg["pnb0"] = nc.dram_tensor("dbg_pnb0", [128, 1152], BF16, kind="ExternalOutput")
        dbg["afm0"] = nc.dram_tensor("dbg_afm0", [128, 2048], BF16, kind="ExternalOutput")
        dbg["rbc0"] = nc.dram_tensor("dbg_rbc0", [128, 128], BF16, kind="ExternalOutput")
        dbg["misc0"] = nc.dram_tensor("dbg_misc0", [128, 512], F32, kind="ExternalOutput")
        dbg["vtm0"] = nc.dram_tensor("dbg_vtm0", [128, 8192], BF16, kind="ExternalOutput")

    def sb(name, shape, dt):
        return es.enter_context(nc.sbuf_tensor(name, shape, dt))

    QF = 1024
    sAbs = sb("sAbs", [128, QF], F32)
    sAy = sb("sAy", [128, QF], F32)
    sC1 = sb("sC1", [128, QF], F32)
    sC2 = sb("sC2", [128, QF], F32)
    sR = sb("sR", [128, QF], F32)
    sT = sb("sT", [128, QF], F32)
    sSgn = sb("sSgn", [128, QF], BF16)
    RSx = sb("RSx", [128, QF], F32)
    Sxx = sb("Sxx", [128, QF], F32)
    amax = sb("amax", [128, 128], F32)
    ebi = sb("ebi", [128, 128], I32)
    bR = sb("bR", [128, 128], I32)
    bS = sb("bS", [128, 128], I32)
    eFs = sb("eFs", [128, 128], F32)
    rFs = sb("rFs", [128, 128], F32)
    lA = sb("lA", [128, 128], F32)
    lC = sb("lC", [128, 128], F32)
    lF = sb("lF", [128, 128], F32)
    SxB = sb("SxB", [128, 128], F32)
    RSxB = sb("RSxB", [128, 128], F32)
    LNSQ = sb("LNSQ", [128, 1024], F32)
    sm = [sb(f"sm{i}", [128, 1], F32) for i in range(10)]
    eye = sb("eye", [128, 128], BF16)
    mlo = sb("mlo", [128, 128], F32)
    mhi = sb("mhi", [128, 128], F32)
    kon = sb("kon", [128, NET], BF16)
    on1 = sb("on1", [1, 128], BF16)
    ptr = [es.enter_context(nc.psum_tensor(f"ptr{i}", [128, 1024], BF16))
           for i in range(2)]

    prog = Prog()
    s_ld = Sem(es.enter_context(nc.semaphore("s_ld")))
    s_gl = Sem(es.enter_context(nc.semaphore("s_gl")))
    s_pe = Sem(es.enter_context(nc.semaphore("s_pe")))
    s_ve = Sem(es.enter_context(nc.semaphore("s_ve")))
    s_ac = Sem(es.enter_context(nc.semaphore("s_ac")))
    s_st = Sem(es.enter_context(nc.semaphore("s_st")))
    s_l0 = Sem(es.enter_context(nc.semaphore("s_l0")))
    s_l1 = Sem(es.enter_context(nc.semaphore("s_l1")))
    s_g0 = Sem(es.enter_context(nc.semaphore("s_g0")))
    s_g1 = Sem(es.enter_context(nc.semaphore("s_g1")))
    s_po = Sem(es.enter_context(nc.semaphore("s_po")))
    ALL_SEMS = [s_ld, s_gl, s_pe, s_ve, s_ac, s_st, s_l0, s_l1, s_g0, s_g1, s_po]

    def fence(engines=ENGINES):
        for e in engines:
            for s in ALL_SEMS:
                prog.wait(e, s, s.n)

    prog.dma("sync", eye[:, :], eye_d[:, :], s_ld)
    prog.dma("sync", mlo[:, :], mlo_d[:, :], s_ld)
    prog.dma("sync", mhi[:, :], mhi_d[:, :], s_ld)
    prog.dma("sync", kon[:, :], kones_d[:, :], s_ld)
    prog.dma("sync", on1[:, :], ones1_d[:, :], s_ld)
    CONST_LD = s_ld.n

    guard = {"q_ve": 0, "q_ac": 0, "ln_ve": 0, "ln_ac": 0, "qp": 0}

    def emit_quant(src_ap, dst_ap, Fc, is_weight, src_ready=(), g_off=0):
        G = Fc // 32
        imq = 0.25 if is_weight else 1.0
        clip = 6.0 if is_weight else 7.5
        prog.wait("scalar", s_ve, guard["q_ve"])
        for s, v in src_ready:
            prog.wait("scalar", s, v)
        prog.op("scalar", lambda E: E.activation(sAbs[:, :Fc], src_ap, AF.Abs))
        a_sgn = prog.op("scalar", lambda E: E.activation(
            sSgn[:, :Fc], src_ap, AF.Sign), s_ac)
        for s, v in src_ready:
            prog.wait("vector", s, v)
        v_red = None
        if is_weight:
            v_red = prog.op("vector", lambda E: E.tensor_reduce(
                amax[:, :G], src_ap.rearrange("p (g k) -> p g k", k=32),
                AX.X, ALU.max, apply_absolute_value=True), s_ve)
            prog.seq("vector", lambda E: E.tensor_scalar(
                ebi[:, :G], amax[:, :G].bitcast(I32), 23, None,
                ALU.logical_shift_right), s_ve)
            prog.seq("vector", lambda E: E.tensor_copy(eFs[:, :G], ebi[:, :G]), s_ve)
            prog.seq("vector", lambda E: E.tensor_scalar(
                eFs[:, :G], eFs[:, :G], 2.0, 95.0, ALU.subtract, ALU.max), s_ve)
            prog.seq("vector", lambda E: E.tensor_scalar(
                eFs[:, :G], eFs[:, :G], 159.0, None, ALU.min), s_ve)
            prog.seq("vector", lambda E: E.tensor_scalar(
                rFs[:, :G], eFs[:, :G], -1.0, 254.0, ALU.mult, ALU.add), s_ve)
            prog.seq("vector", lambda E: E.tensor_copy(bS[:, :G], eFs[:, :G]), s_ve)
            prog.seq("vector", lambda E: E.tensor_copy(bR[:, :G], rFs[:, :G]), s_ve)
            prog.seq("vector", lambda E: E.tensor_scalar(
                bS[:, :G], bS[:, :G], 23, None, ALU.logical_shift_left), s_ve)
            prog.seq("vector", lambda E: E.tensor_scalar(
                bR[:, :G], bR[:, :G], 23, None, ALU.logical_shift_left), s_ve)
            prog.seq("vector", lambda E: E.tensor_copy(
                RSx[:, :Fc].rearrange("p (g k) -> p g k", k=32),
                bR[:, :G].bitcast(F32).unsqueeze(2).broadcast_to([128, G, 32])), s_ve)
            prog.seq("vector", lambda E: E.tensor_copy(
                Sxx[:, :Fc].rearrange("p (g k) -> p g k", k=32),
                bS[:, :G].bitcast(F32).unsqueeze(2).broadcast_to([128, G, 32])), s_ve)
        else:
            prog.seq("vector", lambda E: E.tensor_copy(
                RSx[:, :Fc].rearrange("p (g k) -> p g k", k=32),
                RSxB[:, g_off:g_off + G].unsqueeze(2).broadcast_to(
                    [128, G, 32])), s_ve)
            prog.seq("vector", lambda E: E.tensor_copy(
                Sxx[:, :Fc].rearrange("p (g k) -> p g k", k=32),
                SxB[:, g_off:g_off + G].unsqueeze(2).broadcast_to(
                    [128, G, 32])), s_ve)
        prog.wait("vector", s_ac, a_sgn)
        prog.op("vector", lambda E: E.tensor_tensor(
            sAy[:, :Fc], sAbs[:, :Fc], RSx[:, :Fc], ALU.mult))
        prog.op("vector", lambda E: E.tensor_scalar(
            sC1[:, :Fc], sAy[:, :Fc], 2.0, None, ALU.is_ge))
        prog.op("vector", lambda E: E.tensor_scalar(
            sC2[:, :Fc], sAy[:, :Fc], 4.0, None, ALU.is_ge))
        prog.op("vector", lambda E: E.tensor_scalar(
            sC1[:, :Fc], sC1[:, :Fc], -imq, 2.0 * imq, ALU.mult, ALU.add))
        prog.op("vector", lambda E: E.tensor_scalar(
            sC2[:, :Fc], sC2[:, :Fc], -2.0, 4.0, ALU.mult, ALU.add))
        prog.op("vector", lambda E: E.tensor_tensor(
            sR[:, :Fc], sC1[:, :Fc], sC2[:, :Fc], ALU.mult))
        prog.op("vector", lambda E: E.tensor_tensor(
            sT[:, :Fc], sAy[:, :Fc], sR[:, :Fc], ALU.mult))
        if not is_weight:
            prog.op("vector", lambda E: E.tensor_scalar(
                sC1[:, :Fc], sT[:, :Fc], 8.0, None, ALU.is_ge))
            prog.op("vector", lambda E: E.scalar_tensor_tensor(
                sC2[:, :Fc], sC1[:, :Fc], -8.0, sT[:, :Fc], ALU.mult, ALU.add))
            prog.op("vector", lambda E: E.tensor_scalar(
                sC1[:, :Fc], sC2[:, :Fc], 4.0, None, ALU.is_ge))
            prog.op("vector", lambda E: E.scalar_tensor_tensor(
                sAbs[:, :Fc], sC1[:, :Fc], -4.0, sC2[:, :Fc], ALU.mult, ALU.add))
        else:
            prog.op("vector", lambda E: E.tensor_copy(
                sAbs[:, :Fc], sT[:, :Fc]))
        prog.op("vector", lambda E: E.tensor_scalar(
            sC1[:, :Fc], sAbs[:, :Fc], 2.0, None, ALU.is_ge))
        prog.op("vector", lambda E: E.scalar_tensor_tensor(
            sC2[:, :Fc], sC1[:, :Fc], -2.0, sAbs[:, :Fc], ALU.mult, ALU.add))
        prog.op("vector", lambda E: E.tensor_scalar(
            sC1[:, :Fc], sC2[:, :Fc], 0.5, None, ALU.is_gt))
        prog.op("vector", lambda E: E.tensor_scalar(
            sAbs[:, :Fc], sC2[:, :Fc], 1.5, None, ALU.is_ge))
        prog.op("vector", lambda E: E.tensor_tensor(
            sC1[:, :Fc], sC1[:, :Fc], sAbs[:, :Fc], ALU.add))
        prog.op("vector", lambda E: E.tensor_tensor(
            sC2[:, :Fc], sC2[:, :Fc], sC1[:, :Fc], ALU.subtract))
        prog.op("vector", lambda E: E.tensor_tensor(
            sT[:, :Fc], sT[:, :Fc], sC2[:, :Fc], ALU.subtract))
        prog.op("vector", lambda E: E.reciprocal(sR[:, :Fc], sR[:, :Fc]))
        prog.op("vector", lambda E: E.tensor_tensor(
            sAbs[:, :Fc], sT[:, :Fc], sR[:, :Fc], ALU.mult))
        prog.op("vector", lambda E: E.scalar_tensor_tensor(
            sAbs[:, :Fc], sAbs[:, :Fc], clip, Sxx[:, :Fc], ALU.min, ALU.mult))
        v = prog.op("vector", lambda E: E.tensor_tensor(
            dst_ap, sAbs[:, :Fc], sSgn[:, :Fc], ALU.mult), s_ve)
        guard["q_ve"] = v
        return {"done": (s_ve, v),
                "src_free": [(s_ac, a_sgn), (s_ve, v if v_red is None else v_red)]}

    pscr = {}

    def emit_quant_pool(src_ap, dst_ap, src_ready=()):
        Fc = QF
        G = Fc // 32
        pAbs, pAy, pC1, pC2, pT, pRR, pSgn, pRSx, pSxx = (
            pscr["pAbs"], pscr["pAy"], pscr["pC1"], pscr["pC2"],
            pscr["pT"], pscr["pRR"], pscr["pSgn"], pscr["pRSx"], pscr["pSxx"])
        prog.wait("scalar", s_po, guard["qp"])
        for s, v in src_ready:
            prog.wait("scalar", s, v)
        prog.op("scalar", lambda E: E.activation(pAbs[:, :], src_ap, AF.Abs))
        a_sgn = prog.op("scalar", lambda E: E.activation(
            pSgn[:, :], src_ap, AF.Sign), s_ac)
        for s, v in src_ready:
            prog.wait("vector", s, v)
        v_red = prog.op("vector", lambda E: E.tensor_reduce(
            amax[:, :G], src_ap.rearrange("p (g k) -> p g k", k=32),
            AX.X, ALU.max, apply_absolute_value=True), s_ve)
        prog.seq("vector", lambda E: E.tensor_scalar(
            ebi[:, :G], amax[:, :G].bitcast(I32), 23, None,
            ALU.logical_shift_right), s_ve)
        prog.seq("vector", lambda E: E.tensor_copy(eFs[:, :G], ebi[:, :G]), s_ve)
        prog.seq("vector", lambda E: E.tensor_scalar(
            eFs[:, :G], eFs[:, :G], 2.0, 95.0, ALU.subtract, ALU.max), s_ve)
        prog.seq("vector", lambda E: E.tensor_scalar(
            eFs[:, :G], eFs[:, :G], 159.0, None, ALU.min), s_ve)
        prog.seq("vector", lambda E: E.tensor_scalar(
            rFs[:, :G], eFs[:, :G], -1.0, 254.0, ALU.mult, ALU.add), s_ve)
        prog.seq("vector", lambda E: E.tensor_copy(bS[:, :G], eFs[:, :G]), s_ve)
        prog.seq("vector", lambda E: E.tensor_copy(bR[:, :G], rFs[:, :G]), s_ve)
        prog.seq("vector", lambda E: E.tensor_scalar(
            bS[:, :G], bS[:, :G], 23, None, ALU.logical_shift_left), s_ve)
        prog.seq("vector", lambda E: E.tensor_scalar(
            bR[:, :G], bR[:, :G], 23, None, ALU.logical_shift_left), s_ve)
        prog.wait("vector", s_po, guard["qp"])
        prog.seq("vector", lambda E: E.tensor_copy(
            pRSx[:, :].rearrange("p (g k) -> p g k", k=32),
            bR[:, :G].bitcast(F32).unsqueeze(2).broadcast_to([128, G, 32])), s_ve)
        v_exp = prog.seq("vector", lambda E: E.tensor_copy(
            pSxx[:, :].rearrange("p (g k) -> p g k", k=32),
            bS[:, :G].bitcast(F32).unsqueeze(2).broadcast_to([128, G, 32])), s_ve)
        prog.wait("gpsimd", s_ac, a_sgn)
        prog.wait("gpsimd", s_ve, v_exp)
        prog.op("gpsimd", lambda E: E.tensor_tensor(
            pAy[:, :], pAbs[:, :], pRSx[:, :], ALU.mult))
        prog.op("gpsimd", lambda E: E.tensor_scalar(
            pC1[:, :], pAy[:, :], 2.0, None, ALU.is_ge))
        prog.op("gpsimd", lambda E: E.tensor_scalar(
            pC2[:, :], pAy[:, :], 4.0, None, ALU.is_ge))
        prog.op("gpsimd", lambda E: E.tensor_scalar(
            pRR[:, :], pC2[:, :], 0.5, 0.5, ALU.mult, ALU.add))
        prog.op("gpsimd", lambda E: E.tensor_scalar(
            pT[:, :], pC1[:, :], 1.0, None, ALU.add))
        prog.op("gpsimd", lambda E: E.tensor_tensor(
            pRR[:, :], pT[:, :], pRR[:, :], ALU.mult))
        prog.op("gpsimd", lambda E: E.tensor_scalar(
            pC1[:, :], pC1[:, :], -0.25, 0.5, ALU.mult, ALU.add))
        prog.op("gpsimd", lambda E: E.tensor_scalar(
            pC2[:, :], pC2[:, :], -2.0, 4.0, ALU.mult, ALU.add))
        prog.op("gpsimd", lambda E: E.tensor_tensor(
            pT[:, :], pC1[:, :], pC2[:, :], ALU.mult))
        prog.op("gpsimd", lambda E: E.tensor_tensor(
            pT[:, :], pAy[:, :], pT[:, :], ALU.mult))
        prog.op("gpsimd", lambda E: E.tensor_copy(pAbs[:, :], pT[:, :]))
        prog.op("gpsimd", lambda E: E.tensor_scalar(
            pC1[:, :], pAbs[:, :], 2.0, None, ALU.is_ge))
        prog.op("gpsimd", lambda E: E.tensor_scalar(
            pC2[:, :], pC1[:, :], -2.0, None, ALU.mult))
        prog.op("gpsimd", lambda E: E.tensor_tensor(
            pC2[:, :], pC2[:, :], pAbs[:, :], ALU.add))
        prog.op("gpsimd", lambda E: E.tensor_scalar(
            pC1[:, :], pC2[:, :], 0.5, None, ALU.is_gt))
        prog.op("gpsimd", lambda E: E.tensor_scalar(
            pAbs[:, :], pC2[:, :], 1.5, None, ALU.is_ge))
        prog.op("gpsimd", lambda E: E.tensor_tensor(
            pC1[:, :], pC1[:, :], pAbs[:, :], ALU.add))
        prog.op("gpsimd", lambda E: E.tensor_tensor(
            pC2[:, :], pC2[:, :], pC1[:, :], ALU.subtract))
        prog.op("gpsimd", lambda E: E.tensor_tensor(
            pT[:, :], pT[:, :], pC2[:, :], ALU.subtract))
        prog.op("gpsimd", lambda E: E.tensor_tensor(
            pAbs[:, :], pT[:, :], pRR[:, :], ALU.mult))
        prog.op("gpsimd", lambda E: E.tensor_scalar(
            pAbs[:, :], pAbs[:, :], 6.0, None, ALU.min))
        prog.op("gpsimd", lambda E: E.tensor_tensor(
            pAbs[:, :], pAbs[:, :], pSxx[:, :], ALU.mult))
        v = prog.op("gpsimd", lambda E: E.tensor_tensor(
            dst_ap, pAbs[:, :], pSgn[:, :], ALU.mult), s_po)
        guard["qp"] = v
        return {"done": (s_po, v),
                "src_free": [(s_ac, a_sgn), (s_ve, v_red)]}

    def emit_act_amax(src_ap, Fc, g_off, src_ready=()):
        G = Fc // 32
        for s, v in src_ready:
            prog.wait("vector", s, v)
        return prog.op("vector", lambda E: E.tensor_reduce(
            amax[:, g_off:g_off + G],
            src_ap.rearrange("p (g k) -> p g k", k=32),
            AX.X, ALU.max, apply_absolute_value=True), s_ve)

    def emit_act_scales(Gtot):
        g = slice(0, Gtot)
        prog.seq("vector", lambda E: E.tensor_scalar(
            lA[:, g], amax[:, g], 2.0 ** -32, 2.0 ** 30, ALU.max, ALU.min), s_ve)
        prog.seq("vector", lambda E: E.tensor_scalar(
            lA[:, g], lA[:, g], 2.0 ** 32, None, ALU.mult), s_ve)
        prog.seq("vector", lambda E: E.tensor_scalar(
            lC[:, g], lA[:, g], 2.0 ** 32, None, ALU.is_ge), s_ve)
        prog.seq("vector", lambda E: E.tensor_scalar(
            lF[:, g], lC[:, g], -1.0, 1.0, ALU.mult, ALU.add), s_ve)
        prog.seq("vector", lambda E: E.tensor_scalar(
            SxB[:, g], lC[:, g], 2.0 ** 32, None, ALU.mult), s_ve)
        prog.seq("vector", lambda E: E.tensor_tensor(
            SxB[:, g], SxB[:, g], lF[:, g], ALU.add), s_ve)
        prog.seq("vector", lambda E: E.tensor_scalar(
            rFs[:, g], lC[:, g], 2.0 ** -32, None, ALU.mult), s_ve)
        prog.seq("vector", lambda E: E.tensor_tensor(
            rFs[:, g], rFs[:, g], lF[:, g], ALU.add), s_ve)
        prog.seq("vector", lambda E: E.tensor_tensor(
            lA[:, g], lA[:, g], rFs[:, g], ALU.mult), s_ve)
        for K in (16, 8, 4, 2, 1):
            prog.seq("vector", lambda E, kk=K: E.tensor_scalar(
                lC[:, g], lA[:, g], float(2.0 ** kk), None, ALU.is_ge), s_ve)
            prog.seq("vector", lambda E, kk=K: E.tensor_scalar(
                lF[:, g], lC[:, g], float(2.0 ** -kk - 1.0), 1.0,
                ALU.mult, ALU.add), s_ve)
            prog.seq("vector", lambda E: E.tensor_tensor(
                lA[:, g], lA[:, g], lF[:, g], ALU.mult), s_ve)
            prog.seq("vector", lambda E, kk=K: E.tensor_scalar(
                lF[:, g], lC[:, g], float(2.0 ** kk - 1.0), 1.0,
                ALU.mult, ALU.add), s_ve)
            prog.seq("vector", lambda E: E.tensor_tensor(
                SxB[:, g], SxB[:, g], lF[:, g], ALU.mult), s_ve)
        prog.seq("vector", lambda E: E.tensor_scalar(
            SxB[:, g], SxB[:, g], 2.0 ** -34, None, ALU.mult), s_ve)
        return prog.seq("vector", lambda E: E.reciprocal(
            RSxB[:, g], SxB[:, g]), s_ve)

    def emit_ln(src_ap, dst_ap, src_ready=()):
        prog.wait("scalar", s_ve, guard["ln_ve"])
        for s, v in src_ready:
            prog.wait("scalar", s, v)
        prog.op("scalar", lambda E: E.activation(
            LNSQ[:, :], src_ap[:, 0:1024], AF.Square, accum_out=sm[1][:, :]))
        a_sq = prog.op("scalar", lambda E: E.activation(
            LNSQ[:, :], src_ap[:, 1024:2048], AF.Square, accum_out=sm[8][:, :]),
            s_ac)
        for s, v in src_ready:
            prog.wait("vector", s, v)
        prog.seq("vector", lambda E: E.tensor_reduce(
            sm[0][:, :], src_ap, AX.X, ALU.add), s_ve)
        prog.seq("vector", lambda E: E.tensor_scalar(
            sm[2][:, :], sm[0][:, :], 1.0 / H, None, ALU.mult), s_ve)
        prog.wait("vector", s_ac, a_sq)
        prog.seq("vector", lambda E: E.tensor_tensor(
            sm[3][:, :], sm[1][:, :], sm[8][:, :], ALU.add), s_ve)
        prog.seq("vector", lambda E: E.tensor_scalar(
            sm[3][:, :], sm[3][:, :], 1.0 / H, None, ALU.mult), s_ve)
        prog.seq("vector", lambda E: E.tensor_tensor(
            sm[4][:, :], sm[2][:, :], sm[2][:, :], ALU.mult), s_ve)
        prog.seq("vector", lambda E: E.tensor_tensor(
            sm[4][:, :], sm[3][:, :], sm[4][:, :], ALU.subtract), s_ve)
        v0b = prog.seq("vector", lambda E: E.tensor_scalar(
            sm[4][:, :], sm[4][:, :], 1e-5, None, ALU.add), s_ve)
        prog.wait("scalar", s_ve, v0b)
        a1 = prog.op("scalar", lambda E: E.activation(
            sm[5][:, :], sm[4][:, :], AF.Sqrt), s_ac)
        prog.wait("vector", s_ac, a1)
        v1 = prog.seq("vector", lambda E: E.reciprocal(
            sm[6][:, :], sm[5][:, :]), s_ve)
        prog.wait("vector", s_ve, v1)
        v2 = prog.op("vector", lambda E: E.tensor_scalar(
            dst_ap, src_ap, sm[2][:, :], sm[6][:, :], ALU.subtract, ALU.mult),
            s_ve)
        guard["ln_ve"] = v2
        return {"done": (s_ve, v2),
                "src_free": [(s_ve, v2), (s_ac, a_sq)]}

    tr_state = {"i": 0, "done": [None, None]}

    def emit_transpose(in_ap, out_ap, wait_in=(), ev_wait=()):
        i = tr_state["i"] % 2
        tr_state["i"] += 1
        if tr_state["done"][i] is not None:
            prog.wait("tensor", s_ve, tr_state["done"][i])
        for s, v in wait_in:
            prog.wait("tensor", s, v)
        p = prog.op("tensor", lambda E, a=in_ap, o=ptr[i]: E.transpose(
            o[:, 0:128], a, eye[:, :]), s_pe)
        for s, v in ev_wait:
            prog.wait("vector", s, v)
        prog.wait("vector", s_pe, p)
        v = prog.op("vector", lambda E, o=out_ap, pt=ptr[i]:
                    E.tensor_copy(o, pt[:, 0:128]), s_ve)
        tr_state["done"][i] = v
        return v

    p0es = ExitStack()
    for nm in ("pAbs", "pAy", "pC1", "pC2", "pT", "pRR", "pRSx", "pSxx"):
        pscr[nm] = p0es.enter_context(nc.sbuf_tensor(nm, [128, QF], F32))
    pscr["pSgn"] = p0es.enter_context(nc.sbuf_tensor("pSgn", [128, QF], BF16))
    with (nc.sbuf_tensor("wblk", [128, 2 * I_FF], F32) as wblk,
          nc.sbuf_tensor("qblk", [128, 2 * I_FF], BF16) as qblk):
        last_rd = {}
        last_st = {}
        jobs = []
        for m_d, q_d, nblk, K in ((wg_d, qwg_d, I_FF // 128, H),
                                  (wu_d, qwu_d, I_FF // 128, H),
                                  (wd_d, qwd_d, H // 128, I_FF)):
            for blk in range(nblk):
                jobs.append((m_d, q_d, blk, K))
        ldsem = [s_l0, s_l1]
        stsem = [s_g0, s_g1]
        cnt = 0
        for idx, (m_d, q_d, blk, K) in enumerate(jobs):
            b = idx % 2
            for s, v in last_rd.get(b, ()):
                prog.wait("sync", s, v)
            ld = prog.dma("sync", wblk[:, b * I_FF:b * I_FF + K],
                          m_d[blk * 128:(blk + 1) * 128, :], ldsem[b])
            frees = []
            mils = []
            for ch in range(K // QF):
                o = b * I_FF + ch * QF
                if cnt % 5 in (2, 4):
                    r = emit_quant_pool(wblk[:, o:o + QF], qblk[:, o:o + QF],
                                        src_ready=[(ldsem[b], ld)])
                else:
                    r = emit_quant(wblk[:, o:o + QF], qblk[:, o:o + QF],
                                   QF, True, src_ready=[(ldsem[b], ld)])
                cnt += 1
                frees.extend(r["src_free"])
                mils.append(r["done"])
            last_rd[b] = frees
            if b in last_st:
                prog.wait("gpsimd", stsem[b], last_st[b])
            best = {}
            for s, v in mils:
                best[id(s)] = (s, max(v, best.get(id(s), (s, 0))[1]))
            for s, v in best.values():
                prog.wait("gpsimd", s, v)
            last_st[b] = prog.dma("gpsimd", q_d[blk * 128:(blk + 1) * 128, :],
                                  qblk[:, b * I_FF:b * I_FF + K], stsem[b])
        P0_ST = (s_g0.n, s_g1.n)
    p0es.close()
    if debug:
        prog.wait("gpsimd", s_g0, P0_ST[0])
        prog.wait("gpsimd", s_g1, P0_ST[1])
        prog.dma("gpsimd", dbg["qwg0"][:, :], qwg_d[0:128, :], s_gl)

    fence()

    NB = 6
    pab = ExitStack()
    KFM = pab.enter_context(nc.sbuf_tensor("KFM", [128, KVH * 2048], BF16))
    VTM = pab.enter_context(nc.sbuf_tensor("VTM", [128, NET * 512], BF16))
    QTM = pab.enter_context(nc.sbuf_tensor("QTM", [128, NT * 2048], BF16))
    pa = ExitStack()
    XLT = pa.enter_context(nc.sbuf_tensor("XLT", [128, 16 * 1024], BF16))
    WKB = pa.enter_context(nc.sbuf_tensor("WKB", [128, 16 * 512], BF16))
    WVB = pa.enter_context(nc.sbuf_tensor("WVB", [128, 16 * 512], BF16))
    XT = pa.enter_context(nc.sbuf_tensor("XT", [128, H], F32))
    XLN = pa.enter_context(nc.sbuf_tensor("XLN", [128, H], BF16))
    KTMP = pa.enter_context(nc.sbuf_tensor("KTMP", [128, 512], BF16))
    pp = [pa.enter_context(nc.psum_tensor(f"pp{i}", [128, 512], F32))
          for i in range(NB)]

    for hc in range(16):
        prog.dma("gpsimd", WKB[:, hc * 512:(hc + 1) * 512],
                 wk_d[hc * 128:(hc + 1) * 128, :], s_gl)
        prog.dma("gpsimd", WVB[:, hc * 512:(hc + 1) * 512],
                 wv_d[hc * 128:(hc + 1) * 128, :], s_gl)
    WKV_LD = s_gl.n
    prog.wait("tensor", s_ld, CONST_LD)

    xt_free = {}
    pp_guard = {"v": None}

    def proj(set_i, out_write, rhs_of_hc):
        for b0 in range(0, 8, NB):
            bw = min(NB, 8 - b0)
            if pp_guard["v"] is not None:
                prog.wait("tensor", s_ve, pp_guard["v"])
            last_mm = None
            for hc in range(16):
                for tci in range(bw):
                    tc = b0 + tci
                    lhs = XLT[:, hc * 1024 + tc * 128: hc * 1024 + (tc + 1) * 128]
                    rhs = rhs_of_hc(hc)
                    last_mm = prog.op(
                        "tensor",
                        lambda E, p=pp[tci], l=lhs, r=rhs,
                               st0=(hc == 0), sp=(hc == 15):
                            E.matmul(p[:, :], l, r, start=st0, stop=sp),
                        s_pe if hc == 15 else None)
            for tci in range(bw):
                prog.wait("vector", s_pe, last_mm)
                out_write(b0 + tci, pp[tci])
            pp_guard["v"] = s_ve.n

    for st in range(2):
        for j in range(8):
            et = st * 8 + j
            prog.wait("vector", s_pe, s_pe.n)
            for s, v in xt_free.get(0, ()):
                prog.wait("sync", s, v)
            ld = prog.dma("sync", XT[:, :],
                          x_ext[et * 128:(et + 1) * 128, :], s_ld)
            r = emit_ln(XT[:, :], XLN[:, :], src_ready=[(s_ld, ld)])
            xt_free[0] = r["src_free"]
            for hc in range(16):
                emit_transpose(
                    XLN[:, hc * 128:(hc + 1) * 128],
                    XLT[:, hc * 1024 + j * 128: hc * 1024 + (j + 1) * 128],
                    wait_in=[r["done"]])
        prog.wait("tensor", s_gl, WKV_LD)
        prog.wait("tensor", s_ve, s_ve.n)

        def k_write(tc, ps):
            et = st * 8 + tc
            prog.wait("vector", s_pe, s_pe.n)
            v = prog.op("vector", lambda E, p=ps: E.tensor_copy(
                KTMP[:, :], p[:, :]), s_ve)
            for g in range(KVH):
                emit_transpose(
                    KTMP[:, g * 128:(g + 1) * 128],
                    KFM[:, g * 2048 + et * 128: g * 2048 + (et + 1) * 128],
                    wait_in=[(s_ve, v)])
        proj(st, k_write, lambda hc: WKB[:, hc * 512:(hc + 1) * 512])

        def v_write(tc, ps):
            et = st * 8 + tc
            prog.op("vector", lambda E, p=ps, e=et: E.tensor_copy(
                VTM[:, e * 512:(e + 1) * 512], p[:, :]), s_ve)
        proj(st, v_write, lambda hc: WVB[:, hc * 512:(hc + 1) * 512])

        if st == 1:
            for o5 in range(4):
                prog.wait("gpsimd", s_pe, s_pe.n)
                gl = None
                for hc in range(16):
                    gl = prog.dma("gpsimd", WVB[:, hc * 512:(hc + 1) * 512],
                                  wq_d[hc * 128:(hc + 1) * 128,
                                       o5 * 512:(o5 + 1) * 512], s_gl)
                prog.wait("tensor", s_gl, gl)

                def q_write(tc, ps, o5=o5):
                    prog.op("vector", lambda E, p=ps, t=tc: E.tensor_scalar(
                        QTM[:, t * 2048 + o5 * 512: t * 2048 + (o5 + 1) * 512],
                        p[:, :], INV_SQRT_D, None, ALU.mult), s_ve)
                proj(st, q_write,
                     lambda hc: WVB[:, hc * 512:(hc + 1) * 512])
    if debug:
        prog.wait("gpsimd", s_ve, s_ve.n)
        prog.dma("gpsimd", dbg["xlt0"][:, :], XLT[:, 0:1024], s_st)
        prog.dma("gpsimd", dbg["kfm0"][:, :], KFM[:, 0:2048], s_st)
        prog.dma("gpsimd", dbg["qtm0"][:, :], QTM[:, 0:2048], s_st)
        prog.wait("gpsimd", s_st, s_st.n)
    fence()
    pa.close()

    pb = ExitStack()
    WOB = pb.enter_context(nc.sbuf_tensor("WOB", [128, 16 * 2048], BF16))
    PNBb = pb.enter_context(nc.sbuf_tensor("PNBb", [128, WIN], BF16))
    PTB = pb.enter_context(nc.sbuf_tensor("PTB", [128, 9 * 128], BF16))
    QFM = pb.enter_context(nc.sbuf_tensor("QFM", [128, 16 * 128], BF16))
    AFM = pb.enter_context(nc.sbuf_tensor("AFM", [128, 16 * 128], BF16))
    XRT = pb.enter_context(nc.sbuf_tensor("XRT", [128, H], F32))
    AFMF = pb.enter_context(nc.sbuf_tensor("AFMF", [128, 16 * 128], BF16))
    ps_s = pb.enter_context(nc.psum_tensor("ps_s", [128, 1536], F32))
    ps_wo = pb.enter_context(nc.psum_tensor("ps_wo", [128, 512], F32))
    ps_av = pb.enter_context(nc.psum_tensor("ps_av", [128, 512], F32))
    ps_q1 = ps_av[:, 128:129]

    for fc in range(16):
        prog.dma("gpsimd", WOB[:, fc * 2048:(fc + 1) * 2048],
                 wo_d[fc * 128:(fc + 1) * 128, :], s_gl)
    WO_LD = s_gl.n

    xst = 0
    exp_prev = 0
    tr_prev = 0
    for qt in range(NT):
        prog.wait("vector", s_pe, s_pe.n)
        prog.wait("scalar", s_pe, s_pe.n)
        for h in range(NH):
            emit_transpose(
                QTM[:, qt * 2048 + h * 128: qt * 2048 + (h + 1) * 128],
                QFM[:, h * 128:(h + 1) * 128])
        qfm_ready = s_ve.n
        if xst:
            prog.wait("sync", s_st, xst)
        xld = prog.dma("sync", XRT[:, :],
                       x_ext[(8 + qt) * 128:(9 + qt) * 128, :], s_ld)
        for h in range(NH):
            g = h // 4
            prog.wait("tensor", s_ve, qfm_ready)
            prog.wait("tensor", s_ac, exp_prev)
            kbase = qt * 128
            mm3 = None
            for c0, cw in ((0, 512), (512, 512), (1024, 128)):
                mm3 = prog.op("tensor", lambda E, c=c0, w=cw, hh=h, gg=g, kb=kbase:
                              E.matmul(ps_s[:, c:c + w],
                                       QFM[:, hh * 128:(hh + 1) * 128],
                                       KFM[:, gg * 2048 + kb + c:
                                           gg * 2048 + kb + c + w],
                                       start=True, stop=True),
                              s_pe if c0 == 1024 else None)
            prog.wait("vector", s_pe, mm3)
            prog.wait("vector", s_ld, CONST_LD)
            prog.op("vector", lambda E: E.tensor_tensor(
                ps_s[:, 0:128], ps_s[:, 0:128], mlo[:, :], ALU.add))
            prog.op("vector", lambda E: E.tensor_tensor(
                ps_s[:, 1024:1152], ps_s[:, 1024:1152], mhi[:, :], ALU.add))
            prog.seq("vector", lambda E: E.tensor_reduce(
                sm[0][:, :], ps_s[:, 0:WIN], AX.X, ALU.max), s_ve)
            vmax = prog.seq("vector", lambda E: E.tensor_scalar(
                sm[1][:, :], sm[0][:, :], -1.0, None, ALU.mult), s_ve)
            prog.wait("scalar", s_ve, vmax)
            if debug and qt == 0 and h == 1:
                prog.wait("scalar", s_st, dbg["_pnb_done"])
            prog.wait("scalar", s_pe, tr_prev)
            aex = 0
            for c0, cw in ((0, 512), (512, 512), (1024, 128)):
                aex = prog.op(
                    "scalar", lambda E, c=c0, w=cw:
                        E.activation(PNBb[:, c:c + w], ps_s[:, c:c + w],
                                     AF.Exp, bias=sm[1][:, :]), s_ac)
            exp_prev = aex
            av_prev_val = s_pe.n
            prog.wait("tensor", s_ac, aex)
            for kc in range(9):
                emit_transpose(PNBb[:, kc * 128:(kc + 1) * 128],
                               PTB[:, kc * 128:(kc + 1) * 128],
                               ev_wait=[(s_pe, av_prev_val)])
            tr_prev = s_pe.n
            pt_ready = s_ve.n
            prog.wait("tensor", s_ve, pt_ready)
            smm = None
            avm = None
            for kc in range(9):
                et = qt + kc
                smm = prog.op("tensor", lambda E, e=et, k=kc:
                              E.matmul(ps_q1[:, :],
                                       PTB[:, k * 128:(k + 1) * 128],
                                       kon[:, e:e + 1],
                                       start=(k == 0), stop=(k == 8)),
                              s_pe if kc == 8 else None)
            for kc in range(9):
                et = qt + kc
                avm = prog.op("tensor", lambda E, e=et, k=kc, gg=g:
                              E.matmul(ps_av[:, 0:128],
                                       PTB[:, k * 128:(k + 1) * 128],
                                       VTM[:, e * 512 + gg * 128:
                                           e * 512 + (gg + 1) * 128],
                                       start=(k == 0), stop=(k == 8)),
                              s_pe if kc == 8 else None)
            prog.wait("vector", s_pe, avm)
            vrp = prog.seq("vector", lambda E: E.reciprocal(
                sm[9][:, :], ps_q1[:, :]), s_ve)
            prog.wait("vector", s_ve, vrp)
            prog.op("vector", lambda E, hh=h: E.tensor_scalar(
                AFM[:, hh * 128:(hh + 1) * 128], ps_av[:, 0:128],
                sm[9][:, :], None, ALU.mult), s_ve)
            if debug and qt == 0 and h == 0:
                prog.seq("vector", lambda E: E.tensor_copy(
                    sAy[:, 0:1], ps_q1[:, :]), s_ve)
                prog.seq("vector", lambda E: E.tensor_copy(
                    sAy[:, 384:400], kon[:, :]), s_ve)
                vd = prog.seq("vector", lambda E: E.tensor_copy(
                    sAy[0:1, 400:528], on1[0:1, :]), s_ve)
                prog.wait("gpsimd", s_ve, vd)
                prog.dma("gpsimd", dbg["misc0"][:, :], sAy[:, 0:512], s_st)
                prog.dma("gpsimd", dbg["pnb0"][:, :], PNBb[:, :], s_st)
                prog.dma("gpsimd", dbg["vtm0"][:, :], VTM[:, :], s_st)
                dbg["_pnb_done"] = s_st.n
        if debug and qt == 0:
            prog.wait("gpsimd", s_ve, s_ve.n)
            prog.dma("gpsimd", dbg["afm0"][:, :], AFM[:, :], s_st)
        afm_ready = s_ve.n
        for fc in range(16):
            emit_transpose(AFM[:, fc * 128:(fc + 1) * 128],
                           AFMF[:, fc * 128:(fc + 1) * 128],
                           wait_in=[(s_ve, afm_ready)])
        afmf_ready = s_ve.n
        prog.wait("tensor", s_gl, WO_LD)
        prog.wait("tensor", s_ve, afmf_ready)
        prog.wait("vector", s_ld, xld)
        wo_evac = None
        for o5 in range(4):
            wom = None
            for fc in range(16):
                wom = prog.op("tensor", lambda E, f=fc, o=o5:
                              E.matmul(ps_wo[:, :],
                                       AFMF[:, f * 128:(f + 1) * 128],
                                       WOB[:, f * 2048 + o * 512:
                                           f * 2048 + (o + 1) * 512],
                                       start=(f == 0), stop=(f == 15)),
                              s_pe if fc == 15 else None)
            prog.wait("vector", s_pe, wom)
            wo_evac = prog.op("vector", lambda E, o=o5: E.tensor_tensor(
                XRT[:, o * 512:(o + 1) * 512], ps_wo[:, :],
                XRT[:, o * 512:(o + 1) * 512], ALU.add), s_ve)
            if o5 < 3:
                prog.wait("tensor", s_ve, wo_evac)
        prog.wait("sync", s_ve, wo_evac)
        xst = prog.dma("sync", xres_d[qt * 128:(qt + 1) * 128, :],
                       XRT[:, :], s_st)
    PB_XST = s_st.n
    fence()
    pb.close()
    pab.close()

    pf = ExitStack()
    XRES = pf.enter_context(nc.sbuf_tensor("XRES", [128, NT * H], F32))
    XQT = pf.enter_context(nc.sbuf_tensor("XQT", [128, 16 * 1024], BF16))
    QIT = pf.enter_context(nc.sbuf_tensor("QIT", [128, 8 * 1024], BF16))
    WGT = pf.enter_context(nc.sbuf_tensor("WGT", [128, 16 * 512], BF16))
    WUT = pf.enter_context(nc.sbuf_tensor("WUT", [128, 16 * 512], BF16))

    XQB = pf.enter_context(nc.sbuf_tensor("XQB", [128, H], BF16))
    SGT = pf.enter_context(nc.sbuf_tensor("SGT", [128, 512], F32))
    ITC = pf.enter_context(nc.sbuf_tensor("ITC", [128, 2048], F32))
    XL2 = ITC
    QXC = pf.enter_context(nc.sbuf_tensor("QXC", [128, 1024], BF16))
    psG = pf.enter_context(nc.psum_tensor("psG", [128, 512], F32))
    psU = pf.enter_context(nc.psum_tensor("psU", [128, 512], F32))
    psD = pf.enter_context(nc.psum_tensor("psD", [128, 512], F32))

    prog.wait("sync", s_st, PB_XST)
    xres_ld = None
    for tc in range(NT):
        xres_ld = prog.dma("sync", XRES[:, tc * H:(tc + 1) * H],
                           xres_d[tc * 128:(tc + 1) * 128, :], s_ld)
    dbg_xq_st = 0
    for tc in range(NT):
        prog.wait("vector", s_pe, s_pe.n)
        prog.wait("vector", s_ac, s_ac.n)
        if dbg_xq_st:
            prog.wait("vector", s_st, dbg_xq_st)
        r = emit_ln(XRES[:, tc * H:(tc + 1) * H], XL2[:, :],
                    src_ready=[(s_ld, xres_ld)])
        emit_act_amax(XL2[:, 0:1024], 1024, 0, src_ready=[r["done"]])
        emit_act_amax(XL2[:, 1024:2048], 1024, 32)
        emit_act_scales(64)
        q1 = emit_quant(XL2[:, 0:1024], XQB[:, 0:1024], 1024, False,
                        src_ready=[r["done"]], g_off=0)
        q2 = emit_quant(XL2[:, 1024:2048], XQB[:, 1024:2048], 1024, False,
                        src_ready=[r["done"]], g_off=32)
        if debug and tc == 0:
            prog.wait("gpsimd", *q2["done"])
            prog.wait("gpsimd", *q1["done"])
            dbg_xq_st = prog.dma("gpsimd", dbg["xq0"][:, :], XQB[:, :], s_st)
        for hc in range(16):
            emit_transpose(
                XQB[:, hc * 128:(hc + 1) * 128],
                XQT[:, hc * 1024 + tc * 128: hc * 1024 + (tc + 1) * 128],
                wait_in=[q2["done"] if hc >= 8 else q1["done"]])
    xqt_ready = s_ve.n

    down_mil = None
    for islab in range(8):
        prog.wait("vector", s_pe, s_pe.n)
        for o5i in range(2):
            o5 = islab * 2 + o5i
            prog.wait("sync", s_pe, s_pe.n)
            gl = None
            for hc in range(16):
                prog.dma("sync", WGT[:, hc * 512:(hc + 1) * 512],
                         qwg_d[o5 * 512:(o5 + 1) * 512,
                               hc * 128:(hc + 1) * 128], s_ld, transpose=True)
                gl = prog.dma("sync", WUT[:, hc * 512:(hc + 1) * 512],
                              qwu_d[o5 * 512:(o5 + 1) * 512,
                                    hc * 128:(hc + 1) * 128], s_ld,
                              transpose=True)
            prog.wait("tensor", s_ld, gl)
            prog.wait("tensor", s_ve, xqt_ready)
            for half in range(2):
                itc_mil = None
                for tci in range(4):
                    tc = half * 4 + tci
                    prog.wait("vector", s_ac, s_ac.n)
                    prog.wait("scalar", s_ve, s_ve.n)
                    mmg = mmu = None
                    for hc in range(16):
                        lhs = XQT[:, hc * 1024 + tc * 128: hc * 1024 + (tc + 1) * 128]
                        mmg = prog.op("tensor", lambda E, l=lhs, h2=hc:
                                      E.matmul(psG[:, :], l,
                                               WGT[:, h2 * 512:(h2 + 1) * 512],
                                               start=(h2 == 0), stop=(h2 == 15)),
                                      s_pe if hc == 15 else None)
                    for hc in range(16):
                        lhs = XQT[:, hc * 1024 + tc * 128: hc * 1024 + (tc + 1) * 128]
                        mmu = prog.op("tensor", lambda E, l=lhs, h2=hc:
                                      E.matmul(psU[:, :], l,
                                               WUT[:, h2 * 512:(h2 + 1) * 512],
                                               start=(h2 == 0), stop=(h2 == 15)),
                                      s_pe if hc == 15 else None)
                    prog.wait("scalar", s_pe, mmg)
                    asl = prog.op("scalar", lambda E: E.activation(
                        SGT[:, :], psG[:, :], AF.Silu), s_ac)
                    prog.wait("vector", s_pe, mmu)
                    prog.wait("vector", s_ac, asl)
                    itc_mil = prog.op("vector", lambda E, t=tci: E.tensor_tensor(
                        ITC[:, t * 512:(t + 1) * 512], SGT[:, :], psU[:, :],
                        ALU.mult), s_ve)
                    prog.wait("tensor", s_ac, asl)
                    prog.wait("tensor", s_ve, itc_mil)
                emit_act_amax(ITC[:, :], 2048, 0, src_ready=[(s_ve, itc_mil)])
                emit_act_scales(64)
                for ch in range(2):
                    qm = emit_quant(ITC[:, ch * 1024:(ch + 1) * 1024], QXC[:, :],
                                    1024, False, g_off=ch * 32)
                    for j in range(8):
                        tcl = j // 4
                        jj = j % 4
                        ic = o5i * 4 + jj
                        tcg = half * 4 + ch * 2 + tcl
                        emit_transpose(
                            QXC[:, tcl * 512 + jj * 128:
                                tcl * 512 + (jj + 1) * 128],
                            QIT[:, ic * 1024 + tcg * 128:
                                ic * 1024 + (tcg + 1) * 128],
                            wait_in=[qm["done"]])
        qit_ready = s_ve.n
        for h5 in range(4):
            prog.wait("sync", s_pe, s_pe.n)
            dl = None
            for ici in range(8):
                ic = islab * 8 + ici
                dl = prog.dma("sync", WGT[:, ici * 512:(ici + 1) * 512],
                              qwd_d[h5 * 512:(h5 + 1) * 512,
                                    ic * 128:(ic + 1) * 128], s_ld,
                              transpose=True)
            prog.wait("tensor", s_ld, dl)
            prog.wait("tensor", s_ve, qit_ready)
            for tc in range(NT):
                mmd = None
                for ici in range(8):
                    lhs = QIT[:, ici * 1024 + tc * 128: ici * 1024 + (tc + 1) * 128]
                    mmd = prog.op("tensor", lambda E, l=lhs, i2=ici:
                                  E.matmul(psD[:, :], l,
                                           WGT[:, i2 * 512:(i2 + 1) * 512],
                                           start=(i2 == 0), stop=(i2 == 7)),
                                  s_pe if ici == 7 else None)
                prog.wait("vector", s_pe, mmd)
                down_mil = prog.op(
                    "vector", lambda E, t=tc, h2=h5: E.tensor_tensor(
                        XRES[:, t * H + h2 * 512: t * H + (h2 + 1) * 512],
                        psD[:, :],
                        XRES[:, t * H + h2 * 512: t * H + (h2 + 1) * 512],
                        ALU.add), s_ve)
                prog.wait("tensor", s_ve, down_mil)
    prog.wait("sync", s_ve, down_mil)
    for tc in range(NT):
        prog.dma("sync", out_d[tc * 128:(tc + 1) * 128, :],
                 XRES[:, tc * H:(tc + 1) * H], s_st)
    prog.wait("sync", s_st, s_st.n)
    pf.close()

    with nc.Block() as block:
        @block.sync
        def _(E):
            for fn in prog.steps["sync"]:
                fn(E)

        @block.vector
        def _(E):
            for fn in prog.steps["vector"]:
                fn(E)

        @block.scalar
        def _(E):
            for fn in prog.steps["scalar"]:
                fn(E)

        @block.tensor
        def _(E):
            for fn in prog.steps["tensor"]:
                fn(E)

        @block.gpsimd
        def _(E):
            for fn in prog.steps["gpsimd"]:
                fn(E)

    es.close()
    return nc


def make_inputs(full):
    x = np.asarray(full["L0_attn_hidden_states"], dtype=np.float32).reshape(8192, H)
    arrs = {
        "wq": np.ascontiguousarray(np.asarray(full["L0_attn_Wq"], dtype=np.float32)),
        "wk": np.ascontiguousarray(np.asarray(full["L0_attn_Wk"], dtype=np.float32)),
        "wv": np.ascontiguousarray(np.asarray(full["L0_attn_Wv"], dtype=np.float32)),
        "wo": np.ascontiguousarray(np.asarray(full["L0_attn_Wo"], dtype=np.float32)),
        "wg": np.ascontiguousarray(np.asarray(full["L0_ffn_w_gate"], dtype=np.float32)),
        "wu": np.ascontiguousarray(np.asarray(full["L0_ffn_w_up"], dtype=np.float32)),
        "wd": np.ascontiguousarray(np.asarray(full["L0_ffn_w_down"], dtype=np.float32)),
        "eye_bf": np.eye(128, dtype=ml_dtypes.bfloat16),
        "ones1": np.ones((1, 128), dtype=ml_dtypes.bfloat16),
    }
    i_ = np.arange(128)[:, None]
    j_ = np.arange(128)[None, :]
    arrs["mask_lo"] = np.where(j_ < i_, NEG, 0.0).astype(np.float32)
    arrs["mask_hi"] = np.where(j_ > i_, NEG, 0.0).astype(np.float32)
    in_maps = []
    for c in range(8):
        lo = 1024 * c - 1024
        xe = np.zeros((2048, H), dtype=np.float32)
        a = max(lo, 0)
        xe[a - lo:2048, :] = x[a:lo + 2048, :]
        kones = np.zeros((128, NET), dtype=ml_dtypes.bfloat16)
        for e in range(NET):
            for p in range(128):
                kones[p, e] = 1.0 if (1024 * c - 1024 + e * 128 + p) >= 0 else 0.0
        m = dict(arrs)
        m["x_ext"] = xe
        m["kones"] = kones
        in_maps.append(m)
    return in_maps


def kernel(**inputs):
    nc = build_program(debug=False)
    in_maps = make_inputs(inputs)
    res = run_bass_kernel_spmd(nc, in_maps, core_ids=list(range(8)))
    outs = [np.asarray(r["out"], dtype=np.float32) for r in res.results]
    return np.concatenate(outs, axis=0).reshape(1, 8192, H)


# version 07
# speedup vs baseline: 2.0072x; 2.0072x over previous
import math
import numpy as np
import ml_dtypes
import concourse.bass as bass
import concourse.mybir as mybir
from concourse.bass_utils import run_bass_kernel_spmd
from contextlib import ExitStack

F32 = mybir.dt.float32
I32 = mybir.dt.int32
BF16 = mybir.dt.bfloat16
AF = mybir.ActivationFunctionType
ALU = mybir.AluOpType
AX = mybir.AxisListType

H = 2048
D = 128
NH = 16
KVH = 4
I_FF = 8192
T = 1024
NT = 8
NET = 16
WIN = 1152
INV_SQRT_D = 1.0 / math.sqrt(D)
NEG = -1e9

ENGINES = ("sync", "vector", "scalar", "tensor", "gpsimd")


class Sem:
    __slots__ = ("h", "n")
    def __init__(self, h):
        self.h = h
        self.n = 0


class Prog:
    def __init__(self):
        self.steps = {e: [] for e in ENGINES}

    def on(self, eng, fn):
        self.steps[eng].append(fn)

    def wait(self, eng, sem, val=None):
        v = sem.n if val is None else val
        if v > 0:
            self.on(eng, lambda E, s=sem, vv=v: E.wait_ge(s.h, vv))

    def op(self, eng, f, sem=None, k=1):
        if sem is None:
            self.on(eng, f)
            return None
        sem.n += k
        self.on(eng, lambda E, f=f, s=sem, kk=k: f(E).then_inc(s.h, kk))
        return sem.n

    def dma(self, eng, dst, src, sem, **kw):
        return self.op(eng, lambda E, d=dst, s=src: E.dma_start(d, s, **kw),
                       sem, 16)

    def seq(self, eng, f, sem):
        self.wait(eng, sem, sem.n)
        return self.op(eng, f, sem)


def build_program(debug=False, pbv=2):
    nc = bass.Bass("TRN2", target_bir_lowering=False, detect_race_conditions=False)
    es = ExitStack()

    x_ext = nc.dram_tensor("x_ext", [2048, H], F32, kind="ExternalInput")
    wq_d = nc.dram_tensor("wq", [H, H], F32, kind="ExternalInput")
    wk_d = nc.dram_tensor("wk", [H, 512], F32, kind="ExternalInput")
    wv_d = nc.dram_tensor("wv", [H, 512], F32, kind="ExternalInput")
    wo_d = nc.dram_tensor("wo", [H, H], F32, kind="ExternalInput")
    wg_d = nc.dram_tensor("wg", [I_FF, H], F32, kind="ExternalInput")
    wu_d = nc.dram_tensor("wu", [I_FF, H], F32, kind="ExternalInput")
    wd_d = nc.dram_tensor("wd", [H, I_FF], F32, kind="ExternalInput")
    eye_d = nc.dram_tensor("eye_bf", [128, 128], BF16, kind="ExternalInput")
    mlo_d = nc.dram_tensor("mask_lo", [128, 128], F32, kind="ExternalInput")
    mhi_d = nc.dram_tensor("mask_hi", [128, 128], F32, kind="ExternalInput")
    kones_d = nc.dram_tensor("kones", [128, NET], BF16, kind="ExternalInput")
    ones1_d = nc.dram_tensor("ones1", [1, 128], BF16, kind="ExternalInput")
    out_d = nc.dram_tensor("out", [T, H], F32, kind="ExternalOutput")

    qwg_d = nc.dram_tensor("qwg", [I_FF, H], BF16)
    qwu_d = nc.dram_tensor("qwu", [I_FF, H], BF16)
    qwd_d = nc.dram_tensor("qwd", [H, I_FF], BF16)
    xres_d = nc.dram_tensor("xres", [T, H], F32,
                            kind="ExternalOutput" if debug else "Internal")
    dbg = {}
    if debug:
        dbg["xq0"] = nc.dram_tensor("dbg_xq0", [128, H], BF16, kind="ExternalOutput")
        dbg["qwg0"] = nc.dram_tensor("dbg_qwg0", [128, H], BF16, kind="ExternalOutput")
        dbg["xlt0"] = nc.dram_tensor("dbg_xlt0", [128, 1024], BF16, kind="ExternalOutput")
        dbg["kfm0"] = nc.dram_tensor("dbg_kfm0", [128, 2048], BF16, kind="ExternalOutput")
        dbg["qtm0"] = nc.dram_tensor("dbg_qtm0", [128, 2048], BF16, kind="ExternalOutput")
        dbg["pnb0"] = nc.dram_tensor("dbg_pnb0", [128, 1152], BF16, kind="ExternalOutput")
        dbg["afm0"] = nc.dram_tensor("dbg_afm0", [128, 2048], BF16, kind="ExternalOutput")
        dbg["rbc0"] = nc.dram_tensor("dbg_rbc0", [128, 128], BF16, kind="ExternalOutput")
        dbg["misc0"] = nc.dram_tensor("dbg_misc0", [128, 512], F32, kind="ExternalOutput")
        dbg["vtm0"] = nc.dram_tensor("dbg_vtm0", [128, 8192], BF16, kind="ExternalOutput")

    def sb(name, shape, dt):
        return es.enter_context(nc.sbuf_tensor(name, shape, dt))

    QF = 1024
    sAbs = sb("sAbs", [128, QF], F32)
    sAy = sb("sAy", [128, QF], F32)
    sC1 = sb("sC1", [128, QF], F32)
    sC2 = sb("sC2", [128, QF], F32)
    sR = sb("sR", [128, QF], F32)
    sT = sb("sT", [128, QF], F32)
    sSgn = sb("sSgn", [128, QF], BF16)
    RSx = sb("RSx", [128, QF], F32)
    Sxx = sb("Sxx", [128, QF], F32)
    amax = sb("amax", [128, 128], F32)
    ebi = sb("ebi", [128, 128], I32)
    bR = sb("bR", [128, 128], I32)
    bS = sb("bS", [128, 128], I32)
    eFs = sb("eFs", [128, 128], F32)
    rFs = sb("rFs", [128, 128], F32)
    lA = sb("lA", [128, 128], F32)
    lC = sb("lC", [128, 128], F32)
    lF = sb("lF", [128, 128], F32)
    SxB = sb("SxB", [128, 128], F32)
    RSxB = sb("RSxB", [128, 128], F32)
    LNSQ = sb("LNSQ", [128, 1024], F32)
    sm = [sb(f"sm{i}", [128, 1], F32) for i in range(10)]
    eye = sb("eye", [128, 128], BF16)
    mlo = sb("mlo", [128, 128], F32)
    mhi = sb("mhi", [128, 128], F32)
    kon = sb("kon", [128, NET], BF16)
    on1 = sb("on1", [1, 128], BF16)
    ptr = [es.enter_context(nc.psum_tensor(f"ptr{i}", [128, 1024], BF16))
           for i in range(2)]

    prog = Prog()
    s_ld = Sem(es.enter_context(nc.semaphore("s_ld")))
    s_gl = Sem(es.enter_context(nc.semaphore("s_gl")))
    s_pe = Sem(es.enter_context(nc.semaphore("s_pe")))
    s_ve = Sem(es.enter_context(nc.semaphore("s_ve")))
    s_ac = Sem(es.enter_context(nc.semaphore("s_ac")))
    s_st = Sem(es.enter_context(nc.semaphore("s_st")))
    s_l0 = Sem(es.enter_context(nc.semaphore("s_l0")))
    s_l1 = Sem(es.enter_context(nc.semaphore("s_l1")))
    s_g0 = Sem(es.enter_context(nc.semaphore("s_g0")))
    s_g1 = Sem(es.enter_context(nc.semaphore("s_g1")))
    ALL_SEMS = [s_ld, s_gl, s_pe, s_ve, s_ac, s_st, s_l0, s_l1, s_g0, s_g1]

    def fence(engines=ENGINES):
        for e in engines:
            for s in ALL_SEMS:
                prog.wait(e, s, s.n)

    prog.dma("sync", eye[:, :], eye_d[:, :], s_ld)
    prog.dma("sync", mlo[:, :], mlo_d[:, :], s_ld)
    prog.dma("sync", mhi[:, :], mhi_d[:, :], s_ld)
    prog.dma("sync", kon[:, :], kones_d[:, :], s_ld)
    prog.dma("sync", on1[:, :], ones1_d[:, :], s_ld)
    CONST_LD = s_ld.n

    guard = {"q_ve": 0, "q_ac": 0, "ln_ve": 0, "ln_ac": 0}

    def emit_quant(src_ap, dst_ap, Fc, is_weight, src_ready=(), g_off=0):
        G = Fc // 32
        imq = 0.25 if is_weight else 1.0
        clip = 6.0 if is_weight else 7.5
        prog.wait("scalar", s_ve, guard["q_ve"])
        for s, v in src_ready:
            prog.wait("scalar", s, v)
        prog.op("scalar", lambda E: E.activation(sAbs[:, :Fc], src_ap, AF.Abs))
        a_sgn = prog.op("scalar", lambda E: E.activation(
            sSgn[:, :Fc], src_ap, AF.Sign), s_ac)
        for s, v in src_ready:
            prog.wait("vector", s, v)
        v_red = None
        if is_weight:
            v_red = prog.op("vector", lambda E: E.tensor_reduce(
                amax[:, :G], src_ap.rearrange("p (g k) -> p g k", k=32),
                AX.X, ALU.max, apply_absolute_value=True), s_ve)
            prog.seq("vector", lambda E: E.tensor_scalar(
                ebi[:, :G], amax[:, :G].bitcast(I32), 23, None,
                ALU.logical_shift_right), s_ve)
            prog.seq("vector", lambda E: E.tensor_copy(eFs[:, :G], ebi[:, :G]), s_ve)
            prog.seq("vector", lambda E: E.tensor_scalar(
                eFs[:, :G], eFs[:, :G], 2.0, 95.0, ALU.subtract, ALU.max), s_ve)
            prog.seq("vector", lambda E: E.tensor_scalar(
                eFs[:, :G], eFs[:, :G], 159.0, None, ALU.min), s_ve)
            prog.seq("vector", lambda E: E.tensor_scalar(
                rFs[:, :G], eFs[:, :G], -1.0, 254.0, ALU.mult, ALU.add), s_ve)
            prog.seq("vector", lambda E: E.tensor_copy(bS[:, :G], eFs[:, :G]), s_ve)
            prog.seq("vector", lambda E: E.tensor_copy(bR[:, :G], rFs[:, :G]), s_ve)
            prog.seq("vector", lambda E: E.tensor_scalar(
                bS[:, :G], bS[:, :G], 23, None, ALU.logical_shift_left), s_ve)
            prog.seq("vector", lambda E: E.tensor_scalar(
                bR[:, :G], bR[:, :G], 23, None, ALU.logical_shift_left), s_ve)
            rsx_b = bR[:, :G].bitcast(F32).unsqueeze(2).broadcast_to([128, G, 32])
            sxx_b = bS[:, :G].bitcast(F32).unsqueeze(2).broadcast_to([128, G, 32])
        else:
            rsx_b = RSxB[:, g_off:g_off + G].unsqueeze(2).broadcast_to([128, G, 32])
            sxx_b = SxB[:, g_off:g_off + G].unsqueeze(2).broadcast_to([128, G, 32])
        prog.wait("vector", s_ve, s_ve.n)
        prog.wait("vector", s_ac, a_sgn)
        prog.op("vector", lambda E: E.tensor_tensor(
            sAy[:, :Fc].rearrange("p (g k) -> p g k", k=32),
            sAbs[:, :Fc].rearrange("p (g k) -> p g k", k=32),
            rsx_b, ALU.mult))
        prog.op("vector", lambda E: E.tensor_scalar(
            sC1[:, :Fc], sAy[:, :Fc], 2.0, None, ALU.is_ge))
        prog.op("vector", lambda E: E.tensor_scalar(
            sC2[:, :Fc], sAy[:, :Fc], 4.0, None, ALU.is_ge))
        prog.op("vector", lambda E: E.tensor_scalar(
            sC1[:, :Fc], sC1[:, :Fc], -imq, 2.0 * imq, ALU.mult, ALU.add))
        prog.op("vector", lambda E: E.tensor_scalar(
            sC2[:, :Fc], sC2[:, :Fc], -2.0, 4.0, ALU.mult, ALU.add))
        prog.op("vector", lambda E: E.tensor_tensor(
            sR[:, :Fc], sC1[:, :Fc], sC2[:, :Fc], ALU.mult))
        prog.op("vector", lambda E: E.tensor_tensor(
            sT[:, :Fc], sAy[:, :Fc], sR[:, :Fc], ALU.mult))
        if not is_weight:
            prog.op("vector", lambda E: E.tensor_scalar(
                sC1[:, :Fc], sT[:, :Fc], 8.0, None, ALU.is_ge))
            prog.op("vector", lambda E: E.scalar_tensor_tensor(
                sC2[:, :Fc], sC1[:, :Fc], -8.0, sT[:, :Fc], ALU.mult, ALU.add))
            prog.op("vector", lambda E: E.tensor_scalar(
                sC1[:, :Fc], sC2[:, :Fc], 4.0, None, ALU.is_ge))
            prog.op("vector", lambda E: E.scalar_tensor_tensor(
                sAbs[:, :Fc], sC1[:, :Fc], -4.0, sC2[:, :Fc], ALU.mult, ALU.add))
        else:
            prog.op("vector", lambda E: E.tensor_copy(
                sAbs[:, :Fc], sT[:, :Fc]))
        prog.op("vector", lambda E: E.tensor_scalar(
            sC1[:, :Fc], sAbs[:, :Fc], 2.0, None, ALU.is_ge))
        prog.op("vector", lambda E: E.scalar_tensor_tensor(
            sC2[:, :Fc], sC1[:, :Fc], -2.0, sAbs[:, :Fc], ALU.mult, ALU.add))
        prog.op("vector", lambda E: E.tensor_scalar(
            sC1[:, :Fc], sC2[:, :Fc], 0.5, None, ALU.is_gt))
        prog.op("vector", lambda E: E.tensor_scalar(
            sAbs[:, :Fc], sC2[:, :Fc], 1.5, None, ALU.is_ge))
        prog.op("vector", lambda E: E.tensor_tensor(
            sC1[:, :Fc], sC1[:, :Fc], sAbs[:, :Fc], ALU.add))
        prog.op("vector", lambda E: E.tensor_tensor(
            sC2[:, :Fc], sC2[:, :Fc], sC1[:, :Fc], ALU.subtract))
        prog.op("vector", lambda E: E.tensor_tensor(
            sT[:, :Fc], sT[:, :Fc], sC2[:, :Fc], ALU.subtract))
        prog.op("vector", lambda E: E.reciprocal(sR[:, :Fc], sR[:, :Fc]))
        prog.op("vector", lambda E: E.tensor_tensor(
            sAbs[:, :Fc], sT[:, :Fc], sR[:, :Fc], ALU.mult))
        prog.op("vector", lambda E: E.tensor_scalar(
            sAbs[:, :Fc], sAbs[:, :Fc], clip, None, ALU.min))
        prog.op("vector", lambda E: E.tensor_tensor(
            sAbs[:, :Fc].rearrange("p (g k) -> p g k", k=32),
            sAbs[:, :Fc].rearrange("p (g k) -> p g k", k=32),
            sxx_b, ALU.mult))
        v = prog.op("vector", lambda E: E.tensor_tensor(
            dst_ap, sAbs[:, :Fc], sSgn[:, :Fc], ALU.mult), s_ve)
        guard["q_ve"] = v
        return {"done": (s_ve, v),
                "src_free": [(s_ac, a_sgn), (s_ve, v if v_red is None else v_red)]}

    def emit_act_amax(src_ap, Fc, g_off, src_ready=()):
        G = Fc // 32
        for s, v in src_ready:
            prog.wait("vector", s, v)
        return prog.op("vector", lambda E: E.tensor_reduce(
            amax[:, g_off:g_off + G],
            src_ap.rearrange("p (g k) -> p g k", k=32),
            AX.X, ALU.max, apply_absolute_value=True), s_ve)

    def emit_act_scales(Gtot):
        g = slice(0, Gtot)
        prog.seq("vector", lambda E: E.tensor_scalar(
            lA[:, g], amax[:, g], 2.0 ** -32, 2.0 ** 30, ALU.max, ALU.min), s_ve)
        prog.seq("vector", lambda E: E.tensor_scalar(
            lA[:, g], lA[:, g], 2.0 ** 32, None, ALU.mult), s_ve)
        prog.seq("vector", lambda E: E.tensor_scalar(
            lC[:, g], lA[:, g], 2.0 ** 32, None, ALU.is_ge), s_ve)
        prog.seq("vector", lambda E: E.tensor_scalar(
            lF[:, g], lC[:, g], -1.0, 1.0, ALU.mult, ALU.add), s_ve)
        prog.seq("vector", lambda E: E.tensor_scalar(
            SxB[:, g], lC[:, g], 2.0 ** 32, None, ALU.mult), s_ve)
        prog.seq("vector", lambda E: E.tensor_tensor(
            SxB[:, g], SxB[:, g], lF[:, g], ALU.add), s_ve)
        prog.seq("vector", lambda E: E.tensor_scalar(
            rFs[:, g], lC[:, g], 2.0 ** -32, None, ALU.mult), s_ve)
        prog.seq("vector", lambda E: E.tensor_tensor(
            rFs[:, g], rFs[:, g], lF[:, g], ALU.add), s_ve)
        prog.seq("vector", lambda E: E.tensor_tensor(
            lA[:, g], lA[:, g], rFs[:, g], ALU.mult), s_ve)
        for K in (16, 8, 4, 2, 1):
            prog.seq("vector", lambda E, kk=K: E.tensor_scalar(
                lC[:, g], lA[:, g], float(2.0 ** kk), None, ALU.is_ge), s_ve)
            prog.seq("vector", lambda E, kk=K: E.tensor_scalar(
                lF[:, g], lC[:, g], float(2.0 ** -kk - 1.0), 1.0,
                ALU.mult, ALU.add), s_ve)
            prog.seq("vector", lambda E: E.tensor_tensor(
                lA[:, g], lA[:, g], lF[:, g], ALU.mult), s_ve)
            prog.seq("vector", lambda E, kk=K: E.tensor_scalar(
                lF[:, g], lC[:, g], float(2.0 ** kk - 1.0), 1.0,
                ALU.mult, ALU.add), s_ve)
            prog.seq("vector", lambda E: E.tensor_tensor(
                SxB[:, g], SxB[:, g], lF[:, g], ALU.mult), s_ve)
        prog.seq("vector", lambda E: E.tensor_scalar(
            SxB[:, g], SxB[:, g], 2.0 ** -34, None, ALU.mult), s_ve)
        return prog.seq("vector", lambda E: E.reciprocal(
            RSxB[:, g], SxB[:, g]), s_ve)

    def emit_ln(src_ap, dst_ap, src_ready=()):
        prog.wait("scalar", s_ve, guard["ln_ve"])
        for s, v in src_ready:
            prog.wait("scalar", s, v)
        prog.op("scalar", lambda E: E.activation(
            LNSQ[:, :], src_ap[:, 0:1024], AF.Square, accum_out=sm[1][:, :]))
        a_sq = prog.op("scalar", lambda E: E.activation(
            LNSQ[:, :], src_ap[:, 1024:2048], AF.Square, accum_out=sm[8][:, :]),
            s_ac)
        for s, v in src_ready:
            prog.wait("vector", s, v)
        prog.seq("vector", lambda E: E.tensor_reduce(
            sm[0][:, :], src_ap, AX.X, ALU.add), s_ve)
        prog.seq("vector", lambda E: E.tensor_scalar(
            sm[2][:, :], sm[0][:, :], 1.0 / H, None, ALU.mult), s_ve)
        prog.wait("vector", s_ac, a_sq)
        prog.seq("vector", lambda E: E.tensor_tensor(
            sm[3][:, :], sm[1][:, :], sm[8][:, :], ALU.add), s_ve)
        prog.seq("vector", lambda E: E.tensor_scalar(
            sm[3][:, :], sm[3][:, :], 1.0 / H, None, ALU.mult), s_ve)
        prog.seq("vector", lambda E: E.tensor_tensor(
            sm[4][:, :], sm[2][:, :], sm[2][:, :], ALU.mult), s_ve)
        prog.seq("vector", lambda E: E.tensor_tensor(
            sm[4][:, :], sm[3][:, :], sm[4][:, :], ALU.subtract), s_ve)
        v0b = prog.seq("vector", lambda E: E.tensor_scalar(
            sm[4][:, :], sm[4][:, :], 1e-5, None, ALU.add), s_ve)
        prog.wait("scalar", s_ve, v0b)
        a1 = prog.op("scalar", lambda E: E.activation(
            sm[5][:, :], sm[4][:, :], AF.Sqrt), s_ac)
        prog.wait("vector", s_ac, a1)
        v1 = prog.seq("vector", lambda E: E.reciprocal(
            sm[6][:, :], sm[5][:, :]), s_ve)
        prog.wait("vector", s_ve, v1)
        v2 = prog.op("vector", lambda E: E.tensor_scalar(
            dst_ap, src_ap, sm[2][:, :], sm[6][:, :], ALU.subtract, ALU.mult),
            s_ve)
        guard["ln_ve"] = v2
        return {"done": (s_ve, v2),
                "src_free": [(s_ve, v2), (s_ac, a_sq)]}

    tr_state = {"i": 0, "done": [None, None]}

    def emit_transpose(in_ap, out_ap, wait_in=(), ev_wait=()):
        i = tr_state["i"] % 2
        tr_state["i"] += 1
        if tr_state["done"][i] is not None:
            prog.wait("tensor", s_ve, tr_state["done"][i])
        for s, v in wait_in:
            prog.wait("tensor", s, v)
        p = prog.op("tensor", lambda E, a=in_ap, o=ptr[i]: E.transpose(
            o[:, 0:128], a, eye[:, :]), s_pe)
        for s, v in ev_wait:
            prog.wait("vector", s, v)
        prog.wait("vector", s_pe, p)
        v = prog.op("vector", lambda E, o=out_ap, pt=ptr[i]:
                    E.tensor_copy(o, pt[:, 0:128]), s_ve)
        tr_state["done"][i] = v
        return v

    with (nc.sbuf_tensor("wblk", [128, 2 * I_FF], F32) as wblk,
          nc.sbuf_tensor("qblk", [128, 2 * I_FF], BF16) as qblk):
        last_rd = {}
        last_st = {}
        jobs = []
        for m_d, q_d, nblk, K in ((wg_d, qwg_d, I_FF // 128, H),
                                  (wu_d, qwu_d, I_FF // 128, H),
                                  (wd_d, qwd_d, H // 128, I_FF)):
            for blk in range(nblk):
                jobs.append((m_d, q_d, blk, K))
        ldsem = [s_l0, s_l1]
        stsem = [s_g0, s_g1]
        for idx, (m_d, q_d, blk, K) in enumerate(jobs):
            b = idx % 2
            for s, v in last_rd.get(b, ()):
                prog.wait("sync", s, v)
            ld = prog.dma("sync", wblk[:, b * I_FF:b * I_FF + K],
                          m_d[blk * 128:(blk + 1) * 128, :], ldsem[b])
            frees = []
            mil = None
            for ch in range(K // QF):
                o = b * I_FF + ch * QF
                r = emit_quant(wblk[:, o:o + QF], qblk[:, o:o + QF],
                               QF, True, src_ready=[(ldsem[b], ld)])
                frees.extend(r["src_free"])
                mil = r["done"]
            last_rd[b] = frees
            if b in last_st:
                prog.wait("gpsimd", stsem[b], last_st[b])
            prog.wait("gpsimd", *mil)
            last_st[b] = prog.dma("gpsimd", q_d[blk * 128:(blk + 1) * 128, :],
                                  qblk[:, b * I_FF:b * I_FF + K], stsem[b])
        P0_ST = (s_g0.n, s_g1.n)
    if debug:
        prog.wait("gpsimd", s_g0, P0_ST[0])
        prog.wait("gpsimd", s_g1, P0_ST[1])
        prog.dma("gpsimd", dbg["qwg0"][:, :], qwg_d[0:128, :], s_gl)

    fence()

    NB = 6
    pab = ExitStack()
    KFM = pab.enter_context(nc.sbuf_tensor("KFM", [128, KVH * 2048], BF16))
    VTM = pab.enter_context(nc.sbuf_tensor("VTM", [128, NET * 512], BF16))
    QTM = pab.enter_context(nc.sbuf_tensor("QTM", [128, NT * 2048], BF16))
    pa = ExitStack()
    XLT = pa.enter_context(nc.sbuf_tensor("XLT", [128, 16 * 1024], BF16))
    WKB = pa.enter_context(nc.sbuf_tensor("WKB", [128, 16 * 512], BF16))
    WVB = pa.enter_context(nc.sbuf_tensor("WVB", [128, 16 * 512], BF16))
    XT = pa.enter_context(nc.sbuf_tensor("XT", [128, H], F32))
    XLN = pa.enter_context(nc.sbuf_tensor("XLN", [128, H], BF16))
    KTMP = pa.enter_context(nc.sbuf_tensor("KTMP", [128, 512], BF16))
    pp = [pa.enter_context(nc.psum_tensor(f"pp{i}", [128, 512], F32))
          for i in range(NB)]

    for hc in range(16):
        prog.dma("gpsimd", WKB[:, hc * 512:(hc + 1) * 512],
                 wk_d[hc * 128:(hc + 1) * 128, :], s_gl)
        prog.dma("gpsimd", WVB[:, hc * 512:(hc + 1) * 512],
                 wv_d[hc * 128:(hc + 1) * 128, :], s_gl)
    WKV_LD = s_gl.n
    prog.wait("tensor", s_ld, CONST_LD)

    xt_free = {}
    pp_guard = {"v": None}

    def proj(set_i, out_write, rhs_of_hc):
        for b0 in range(0, 8, NB):
            bw = min(NB, 8 - b0)
            if pp_guard["v"] is not None:
                prog.wait("tensor", s_ve, pp_guard["v"])
            last_mm = None
            for hc in range(16):
                for tci in range(bw):
                    tc = b0 + tci
                    lhs = XLT[:, hc * 1024 + tc * 128: hc * 1024 + (tc + 1) * 128]
                    rhs = rhs_of_hc(hc)
                    last_mm = prog.op(
                        "tensor",
                        lambda E, p=pp[tci], l=lhs, r=rhs,
                               st0=(hc == 0), sp=(hc == 15):
                            E.matmul(p[:, :], l, r, start=st0, stop=sp),
                        s_pe if hc == 15 else None)
            for tci in range(bw):
                prog.wait("vector", s_pe, last_mm)
                out_write(b0 + tci, pp[tci])
            pp_guard["v"] = s_ve.n

    for st in range(2):
        for j in range(8):
            et = st * 8 + j
            prog.wait("vector", s_pe, s_pe.n)
            for s, v in xt_free.get(0, ()):
                prog.wait("sync", s, v)
            ld = prog.dma("sync", XT[:, :],
                          x_ext[et * 128:(et + 1) * 128, :], s_ld)
            r = emit_ln(XT[:, :], XLN[:, :], src_ready=[(s_ld, ld)])
            xt_free[0] = r["src_free"]
            for hc in range(16):
                emit_transpose(
                    XLN[:, hc * 128:(hc + 1) * 128],
                    XLT[:, hc * 1024 + j * 128: hc * 1024 + (j + 1) * 128],
                    wait_in=[r["done"]])
        prog.wait("tensor", s_gl, WKV_LD)
        prog.wait("tensor", s_ve, s_ve.n)

        def k_write(tc, ps):
            et = st * 8 + tc
            prog.wait("vector", s_pe, s_pe.n)
            v = prog.op("vector", lambda E, p=ps: E.tensor_copy(
                KTMP[:, :], p[:, :]), s_ve)
            for g in range(KVH):
                emit_transpose(
                    KTMP[:, g * 128:(g + 1) * 128],
                    KFM[:, g * 2048 + et * 128: g * 2048 + (et + 1) * 128],
                    wait_in=[(s_ve, v)])
        proj(st, k_write, lambda hc: WKB[:, hc * 512:(hc + 1) * 512])

        def v_write(tc, ps):
            et = st * 8 + tc
            prog.op("vector", lambda E, p=ps, e=et: E.tensor_copy(
                VTM[:, e * 512:(e + 1) * 512], p[:, :]), s_ve)
        proj(st, v_write, lambda hc: WVB[:, hc * 512:(hc + 1) * 512])

        if st == 1:
            for o5 in range(4):
                prog.wait("gpsimd", s_pe, s_pe.n)
                gl = None
                for hc in range(16):
                    gl = prog.dma("gpsimd", WVB[:, hc * 512:(hc + 1) * 512],
                                  wq_d[hc * 128:(hc + 1) * 128,
                                       o5 * 512:(o5 + 1) * 512], s_gl)
                prog.wait("tensor", s_gl, gl)

                def q_write(tc, ps, o5=o5):
                    prog.op("vector", lambda E, p=ps, t=tc: E.tensor_scalar(
                        QTM[:, t * 2048 + o5 * 512: t * 2048 + (o5 + 1) * 512],
                        p[:, :], INV_SQRT_D, None, ALU.mult), s_ve)
                proj(st, q_write,
                     lambda hc: WVB[:, hc * 512:(hc + 1) * 512])
    if debug:
        prog.wait("gpsimd", s_ve, s_ve.n)
        prog.dma("gpsimd", dbg["xlt0"][:, :], XLT[:, 0:1024], s_st)
        prog.dma("gpsimd", dbg["kfm0"][:, :], KFM[:, 0:2048], s_st)
        prog.dma("gpsimd", dbg["qtm0"][:, :], QTM[:, 0:2048], s_st)
        prog.wait("gpsimd", s_st, s_st.n)
    fence()
    pa.close()

    pb = ExitStack()
    WOB = pb.enter_context(nc.sbuf_tensor("WOB", [128, 16 * 2048], BF16))
    PNBb = pb.enter_context(nc.sbuf_tensor("PNBb", [128, WIN], BF16))
    PTB = pb.enter_context(nc.sbuf_tensor("PTB", [128, 9 * 128], BF16))
    QFM = pb.enter_context(nc.sbuf_tensor("QFM", [128, 16 * 128], BF16))
    AFM = pb.enter_context(nc.sbuf_tensor("AFM", [128, 16 * 128], BF16))
    XRT = pb.enter_context(nc.sbuf_tensor("XRT", [128, H], F32))
    AFMF = pb.enter_context(nc.sbuf_tensor("AFMF", [128, 16 * 128], BF16))
    ps_s = pb.enter_context(nc.psum_tensor("ps_s", [128, 1536], F32))
    ps_wo = pb.enter_context(nc.psum_tensor("ps_wo", [128, 512], F32))
    ps_av = pb.enter_context(nc.psum_tensor("ps_av", [128, 512], F32))
    ps_q1 = ps_av[:, 128:129]

    for fc in range(16):
        prog.dma("gpsimd", WOB[:, fc * 2048:(fc + 1) * 2048],
                 wo_d[fc * 128:(fc + 1) * 128, :], s_gl)
    WO_LD = s_gl.n

    xst = 0
    exp_prev = 0
    tr_prev = 0
    for qt in range(NT):
        prog.wait("vector", s_pe, s_pe.n)
        prog.wait("scalar", s_pe, s_pe.n)
        for h in range(NH):
            emit_transpose(
                QTM[:, qt * 2048 + h * 128: qt * 2048 + (h + 1) * 128],
                QFM[:, h * 128:(h + 1) * 128])
        qfm_ready = s_ve.n
        if xst:
            prog.wait("sync", s_st, xst)
        xld = prog.dma("sync", XRT[:, :],
                       x_ext[(8 + qt) * 128:(9 + qt) * 128, :], s_ld)
        for h in range(NH):
            g = h // 4
            prog.wait("tensor", s_ve, qfm_ready)
            prog.wait("tensor", s_ac, exp_prev)
            kbase = qt * 128
            mm3 = None
            for c0, cw in ((0, 512), (512, 512), (1024, 128)):
                mm3 = prog.op("tensor", lambda E, c=c0, w=cw, hh=h, gg=g, kb=kbase:
                              E.matmul(ps_s[:, c:c + w],
                                       QFM[:, hh * 128:(hh + 1) * 128],
                                       KFM[:, gg * 2048 + kb + c:
                                           gg * 2048 + kb + c + w],
                                       start=True, stop=True),
                              s_pe if c0 == 1024 else None)
            prog.wait("vector", s_pe, mm3)
            prog.wait("vector", s_ld, CONST_LD)
            prog.op("vector", lambda E: E.tensor_tensor(
                ps_s[:, 0:128], ps_s[:, 0:128], mlo[:, :], ALU.add))
            prog.op("vector", lambda E: E.tensor_tensor(
                ps_s[:, 1024:1152], ps_s[:, 1024:1152], mhi[:, :], ALU.add))
            prog.seq("vector", lambda E: E.tensor_reduce(
                sm[0][:, :], ps_s[:, 0:WIN], AX.X, ALU.max), s_ve)
            vmax = prog.seq("vector", lambda E: E.tensor_scalar(
                sm[1][:, :], sm[0][:, :], -1.0, None, ALU.mult), s_ve)
            prog.wait("scalar", s_ve, vmax)
            if debug and qt == 0 and h == 1:
                prog.wait("scalar", s_st, dbg["_pnb_done"])
            prog.wait("scalar", s_pe, tr_prev)
            aex = 0
            for c0, cw in ((0, 512), (512, 512), (1024, 128)):
                aex = prog.op(
                    "scalar", lambda E, c=c0, w=cw:
                        E.activation(PNBb[:, c:c + w], ps_s[:, c:c + w],
                                     AF.Exp, bias=sm[1][:, :]), s_ac)
            exp_prev = aex
            av_prev_val = s_pe.n
            prog.wait("tensor", s_ac, aex)
            for kc in range(9):
                emit_transpose(PNBb[:, kc * 128:(kc + 1) * 128],
                               PTB[:, kc * 128:(kc + 1) * 128],
                               ev_wait=[(s_pe, av_prev_val)])
            tr_prev = s_pe.n
            pt_ready = s_ve.n
            prog.wait("tensor", s_ve, pt_ready)
            smm = None
            avm = None
            for kc in range(9):
                et = qt + kc
                smm = prog.op("tensor", lambda E, e=et, k=kc:
                              E.matmul(ps_q1[:, :],
                                       PTB[:, k * 128:(k + 1) * 128],
                                       kon[:, e:e + 1],
                                       start=(k == 0), stop=(k == 8)),
                              s_pe if kc == 8 else None)
            for kc in range(9):
                et = qt + kc
                avm = prog.op("tensor", lambda E, e=et, k=kc, gg=g:
                              E.matmul(ps_av[:, 0:128],
                                       PTB[:, k * 128:(k + 1) * 128],
                                       VTM[:, e * 512 + gg * 128:
                                           e * 512 + (gg + 1) * 128],
                                       start=(k == 0), stop=(k == 8)),
                              s_pe if kc == 8 else None)
            prog.wait("vector", s_pe, avm)
            vrp = prog.seq("vector", lambda E: E.reciprocal(
                sm[9][:, :], ps_q1[:, :]), s_ve)
            prog.wait("vector", s_ve, vrp)
            prog.op("vector", lambda E, hh=h: E.tensor_scalar(
                AFM[:, hh * 128:(hh + 1) * 128], ps_av[:, 0:128],
                sm[9][:, :], None, ALU.mult), s_ve)
            if debug and qt == 0 and h == 0:
                prog.seq("vector", lambda E: E.tensor_copy(
                    sAy[:, 0:1], ps_q1[:, :]), s_ve)
                prog.seq("vector", lambda E: E.tensor_copy(
                    sAy[:, 384:400], kon[:, :]), s_ve)
                vd = prog.seq("vector", lambda E: E.tensor_copy(
                    sAy[0:1, 400:528], on1[0:1, :]), s_ve)
                prog.wait("gpsimd", s_ve, vd)
                prog.dma("gpsimd", dbg["misc0"][:, :], sAy[:, 0:512], s_st)
                prog.dma("gpsimd", dbg["pnb0"][:, :], PNBb[:, :], s_st)
                prog.dma("gpsimd", dbg["vtm0"][:, :], VTM[:, :], s_st)
                dbg["_pnb_done"] = s_st.n
        if debug and qt == 0:
            prog.wait("gpsimd", s_ve, s_ve.n)
            prog.dma("gpsimd", dbg["afm0"][:, :], AFM[:, :], s_st)
        afm_ready = s_ve.n
        for fc in range(16):
            emit_transpose(AFM[:, fc * 128:(fc + 1) * 128],
                           AFMF[:, fc * 128:(fc + 1) * 128],
                           wait_in=[(s_ve, afm_ready)])
        afmf_ready = s_ve.n
        prog.wait("tensor", s_gl, WO_LD)
        prog.wait("tensor", s_ve, afmf_ready)
        prog.wait("vector", s_ld, xld)
        wo_evac = None
        for o5 in range(4):
            wom = None
            for fc in range(16):
                wom = prog.op("tensor", lambda E, f=fc, o=o5:
                              E.matmul(ps_wo[:, :],
                                       AFMF[:, f * 128:(f + 1) * 128],
                                       WOB[:, f * 2048 + o * 512:
                                           f * 2048 + (o + 1) * 512],
                                       start=(f == 0), stop=(f == 15)),
                              s_pe if fc == 15 else None)
            prog.wait("vector", s_pe, wom)
            wo_evac = prog.op("vector", lambda E, o=o5: E.tensor_tensor(
                XRT[:, o * 512:(o + 1) * 512], ps_wo[:, :],
                XRT[:, o * 512:(o + 1) * 512], ALU.add), s_ve)
            if o5 < 3:
                prog.wait("tensor", s_ve, wo_evac)
        prog.wait("sync", s_ve, wo_evac)
        xst = prog.dma("sync", xres_d[qt * 128:(qt + 1) * 128, :],
                       XRT[:, :], s_st)
    PB_XST = s_st.n
    fence()
    pb.close()
    pab.close()

    pf = ExitStack()
    XRES = pf.enter_context(nc.sbuf_tensor("XRES", [128, NT * H], F32))
    XQT = pf.enter_context(nc.sbuf_tensor("XQT", [128, 16 * 1024], BF16))
    QIT = pf.enter_context(nc.sbuf_tensor("QIT", [128, 8 * 1024], BF16))
    WGT = pf.enter_context(nc.sbuf_tensor("WGT", [128, 16 * 512], BF16))
    WUT = pf.enter_context(nc.sbuf_tensor("WUT", [128, 16 * 512], BF16))

    XQB = pf.enter_context(nc.sbuf_tensor("XQB", [128, H], BF16))
    SGT = pf.enter_context(nc.sbuf_tensor("SGT", [128, 512], F32))
    ITC = pf.enter_context(nc.sbuf_tensor("ITC", [128, 2048], F32))
    XL2 = ITC
    QXC = pf.enter_context(nc.sbuf_tensor("QXC", [128, 1024], BF16))
    psG = pf.enter_context(nc.psum_tensor("psG", [128, 512], F32))
    psU = pf.enter_context(nc.psum_tensor("psU", [128, 512], F32))
    psD = pf.enter_context(nc.psum_tensor("psD", [128, 512], F32))

    prog.wait("sync", s_st, PB_XST)
    xres_ld = None
    for tc in range(NT):
        xres_ld = prog.dma("sync", XRES[:, tc * H:(tc + 1) * H],
                           xres_d[tc * 128:(tc + 1) * 128, :], s_ld)
    dbg_xq_st = 0
    for tc in range(NT):
        prog.wait("vector", s_pe, s_pe.n)
        prog.wait("vector", s_ac, s_ac.n)
        if dbg_xq_st:
            prog.wait("vector", s_st, dbg_xq_st)
        r = emit_ln(XRES[:, tc * H:(tc + 1) * H], XL2[:, :],
                    src_ready=[(s_ld, xres_ld)])
        emit_act_amax(XL2[:, 0:1024], 1024, 0, src_ready=[r["done"]])
        emit_act_amax(XL2[:, 1024:2048], 1024, 32)
        emit_act_scales(64)
        q1 = emit_quant(XL2[:, 0:1024], XQB[:, 0:1024], 1024, False,
                        src_ready=[r["done"]], g_off=0)
        q2 = emit_quant(XL2[:, 1024:2048], XQB[:, 1024:2048], 1024, False,
                        src_ready=[r["done"]], g_off=32)
        if debug and tc == 0:
            prog.wait("gpsimd", *q2["done"])
            prog.wait("gpsimd", *q1["done"])
            dbg_xq_st = prog.dma("gpsimd", dbg["xq0"][:, :], XQB[:, :], s_st)
        for hc in range(16):
            emit_transpose(
                XQB[:, hc * 128:(hc + 1) * 128],
                XQT[:, hc * 1024 + tc * 128: hc * 1024 + (tc + 1) * 128],
                wait_in=[q2["done"] if hc >= 8 else q1["done"]])
    xqt_ready = s_ve.n

    down_mil = None
    for islab in range(8):
        prog.wait("vector", s_pe, s_pe.n)
        for o5i in range(2):
            o5 = islab * 2 + o5i
            prog.wait("sync", s_pe, s_pe.n)
            gl = None
            for hc in range(16):
                prog.dma("sync", WGT[:, hc * 512:(hc + 1) * 512],
                         qwg_d[o5 * 512:(o5 + 1) * 512,
                               hc * 128:(hc + 1) * 128], s_ld, transpose=True)
                gl = prog.dma("sync", WUT[:, hc * 512:(hc + 1) * 512],
                              qwu_d[o5 * 512:(o5 + 1) * 512,
                                    hc * 128:(hc + 1) * 128], s_ld,
                              transpose=True)
            prog.wait("tensor", s_ld, gl)
            prog.wait("tensor", s_ve, xqt_ready)
            for half in range(2):
                itc_mil = None
                for tci in range(4):
                    tc = half * 4 + tci
                    prog.wait("vector", s_ac, s_ac.n)
                    prog.wait("scalar", s_ve, s_ve.n)
                    mmg = mmu = None
                    for hc in range(16):
                        lhs = XQT[:, hc * 1024 + tc * 128: hc * 1024 + (tc + 1) * 128]
                        mmg = prog.op("tensor", lambda E, l=lhs, h2=hc:
                                      E.matmul(psG[:, :], l,
                                               WGT[:, h2 * 512:(h2 + 1) * 512],
                                               start=(h2 == 0), stop=(h2 == 15)),
                                      s_pe if hc == 15 else None)
                    for hc in range(16):
                        lhs = XQT[:, hc * 1024 + tc * 128: hc * 1024 + (tc + 1) * 128]
                        mmu = prog.op("tensor", lambda E, l=lhs, h2=hc:
                                      E.matmul(psU[:, :], l,
                                               WUT[:, h2 * 512:(h2 + 1) * 512],
                                               start=(h2 == 0), stop=(h2 == 15)),
                                      s_pe if hc == 15 else None)
                    prog.wait("scalar", s_pe, mmg)
                    asl = prog.op("scalar", lambda E: E.activation(
                        SGT[:, :], psG[:, :], AF.Silu), s_ac)
                    prog.wait("vector", s_pe, mmu)
                    prog.wait("vector", s_ac, asl)
                    itc_mil = prog.op("vector", lambda E, t=tci: E.tensor_tensor(
                        ITC[:, t * 512:(t + 1) * 512], SGT[:, :], psU[:, :],
                        ALU.mult), s_ve)
                    prog.wait("tensor", s_ac, asl)
                    prog.wait("tensor", s_ve, itc_mil)
                emit_act_amax(ITC[:, :], 2048, 0, src_ready=[(s_ve, itc_mil)])
                emit_act_scales(64)
                for ch in range(2):
                    qm = emit_quant(ITC[:, ch * 1024:(ch + 1) * 1024], QXC[:, :],
                                    1024, False, g_off=ch * 32)
                    for j in range(8):
                        tcl = j // 4
                        jj = j % 4
                        ic = o5i * 4 + jj
                        tcg = half * 4 + ch * 2 + tcl
                        emit_transpose(
                            QXC[:, tcl * 512 + jj * 128:
                                tcl * 512 + (jj + 1) * 128],
                            QIT[:, ic * 1024 + tcg * 128:
                                ic * 1024 + (tcg + 1) * 128],
                            wait_in=[qm["done"]])
        qit_ready = s_ve.n
        for h5 in range(4):
            prog.wait("sync", s_pe, s_pe.n)
            dl = None
            for ici in range(8):
                ic = islab * 8 + ici
                dl = prog.dma("sync", WGT[:, ici * 512:(ici + 1) * 512],
                              qwd_d[h5 * 512:(h5 + 1) * 512,
                                    ic * 128:(ic + 1) * 128], s_ld,
                              transpose=True)
            prog.wait("tensor", s_ld, dl)
            prog.wait("tensor", s_ve, qit_ready)
            for tc in range(NT):
                mmd = None
                for ici in range(8):
                    lhs = QIT[:, ici * 1024 + tc * 128: ici * 1024 + (tc + 1) * 128]
                    mmd = prog.op("tensor", lambda E, l=lhs, i2=ici:
                                  E.matmul(psD[:, :], l,
                                           WGT[:, i2 * 512:(i2 + 1) * 512],
                                           start=(i2 == 0), stop=(i2 == 7)),
                                  s_pe if ici == 7 else None)
                prog.wait("vector", s_pe, mmd)
                down_mil = prog.op(
                    "vector", lambda E, t=tc, h2=h5: E.tensor_tensor(
                        XRES[:, t * H + h2 * 512: t * H + (h2 + 1) * 512],
                        psD[:, :],
                        XRES[:, t * H + h2 * 512: t * H + (h2 + 1) * 512],
                        ALU.add), s_ve)
                prog.wait("tensor", s_ve, down_mil)
    prog.wait("sync", s_ve, down_mil)
    for tc in range(NT):
        prog.dma("sync", out_d[tc * 128:(tc + 1) * 128, :],
                 XRES[:, tc * H:(tc + 1) * H], s_st)
    prog.wait("sync", s_st, s_st.n)
    pf.close()

    with nc.Block() as block:
        @block.sync
        def _(E):
            for fn in prog.steps["sync"]:
                fn(E)

        @block.vector
        def _(E):
            for fn in prog.steps["vector"]:
                fn(E)

        @block.scalar
        def _(E):
            for fn in prog.steps["scalar"]:
                fn(E)

        @block.tensor
        def _(E):
            for fn in prog.steps["tensor"]:
                fn(E)

        @block.gpsimd
        def _(E):
            for fn in prog.steps["gpsimd"]:
                fn(E)

    es.close()
    return nc


def make_inputs(full):
    x = np.asarray(full["L0_attn_hidden_states"], dtype=np.float32).reshape(8192, H)
    arrs = {
        "wq": np.ascontiguousarray(np.asarray(full["L0_attn_Wq"], dtype=np.float32)),
        "wk": np.ascontiguousarray(np.asarray(full["L0_attn_Wk"], dtype=np.float32)),
        "wv": np.ascontiguousarray(np.asarray(full["L0_attn_Wv"], dtype=np.float32)),
        "wo": np.ascontiguousarray(np.asarray(full["L0_attn_Wo"], dtype=np.float32)),
        "wg": np.ascontiguousarray(np.asarray(full["L0_ffn_w_gate"], dtype=np.float32)),
        "wu": np.ascontiguousarray(np.asarray(full["L0_ffn_w_up"], dtype=np.float32)),
        "wd": np.ascontiguousarray(np.asarray(full["L0_ffn_w_down"], dtype=np.float32)),
        "eye_bf": np.eye(128, dtype=ml_dtypes.bfloat16),
        "ones1": np.ones((1, 128), dtype=ml_dtypes.bfloat16),
    }
    i_ = np.arange(128)[:, None]
    j_ = np.arange(128)[None, :]
    arrs["mask_lo"] = np.where(j_ < i_, NEG, 0.0).astype(np.float32)
    arrs["mask_hi"] = np.where(j_ > i_, NEG, 0.0).astype(np.float32)
    in_maps = []
    for c in range(8):
        lo = 1024 * c - 1024
        xe = np.zeros((2048, H), dtype=np.float32)
        a = max(lo, 0)
        xe[a - lo:2048, :] = x[a:lo + 2048, :]
        kones = np.zeros((128, NET), dtype=ml_dtypes.bfloat16)
        for e in range(NET):
            for p in range(128):
                kones[p, e] = 1.0 if (1024 * c - 1024 + e * 128 + p) >= 0 else 0.0
        m = dict(arrs)
        m["x_ext"] = xe
        m["kones"] = kones
        in_maps.append(m)
    return in_maps


def kernel(**inputs):
    nc = build_program(debug=False)
    in_maps = make_inputs(inputs)
    res = run_bass_kernel_spmd(nc, in_maps, core_ids=list(range(8)))
    outs = [np.asarray(r["out"], dtype=np.float32) for r in res.results]
    return np.concatenate(outs, axis=0).reshape(1, 8192, H)


# version 08 (final)
# speedup vs baseline: 2.0322x; 1.0125x over previous
import math
import numpy as np
import ml_dtypes
import concourse.bass as bass
import concourse.mybir as mybir
from concourse.bass_utils import run_bass_kernel_spmd
from contextlib import ExitStack

F32 = mybir.dt.float32
I32 = mybir.dt.int32
BF16 = mybir.dt.bfloat16
AF = mybir.ActivationFunctionType
ALU = mybir.AluOpType
AX = mybir.AxisListType

H = 2048
D = 128
NH = 16
KVH = 4
I_FF = 8192
T = 1024
NT = 8
NET = 16
WIN = 1152
INV_SQRT_D = 1.0 / math.sqrt(D)
NEG = -1e9

ENGINES = ("sync", "vector", "scalar", "tensor", "gpsimd")


class Sem:
    __slots__ = ("h", "n")
    def __init__(self, h):
        self.h = h
        self.n = 0


class Prog:
    def __init__(self):
        self.steps = {e: [] for e in ENGINES}

    def on(self, eng, fn):
        self.steps[eng].append(fn)

    def wait(self, eng, sem, val=None):
        v = sem.n if val is None else val
        if v > 0:
            self.on(eng, lambda E, s=sem, vv=v: E.wait_ge(s.h, vv))

    def op(self, eng, f, sem=None, k=1):
        if sem is None:
            self.on(eng, f)
            return None
        sem.n += k
        self.on(eng, lambda E, f=f, s=sem, kk=k: f(E).then_inc(s.h, kk))
        return sem.n

    def dma(self, eng, dst, src, sem, **kw):
        return self.op(eng, lambda E, d=dst, s=src: E.dma_start(d, s, **kw),
                       sem, 16)

    def seq(self, eng, f, sem):
        self.wait(eng, sem, sem.n)
        return self.op(eng, f, sem)


def build_program(debug=False, pbv=2):
    nc = bass.Bass("TRN2", target_bir_lowering=False, detect_race_conditions=False)
    es = ExitStack()

    x_ext = nc.dram_tensor("x_ext", [2048, H], F32, kind="ExternalInput")
    wq_d = nc.dram_tensor("wq", [H, H], F32, kind="ExternalInput")
    wk_d = nc.dram_tensor("wk", [H, 512], F32, kind="ExternalInput")
    wv_d = nc.dram_tensor("wv", [H, 512], F32, kind="ExternalInput")
    wo_d = nc.dram_tensor("wo", [H, H], F32, kind="ExternalInput")
    wg_d = nc.dram_tensor("wg", [I_FF, H], F32, kind="ExternalInput")
    wu_d = nc.dram_tensor("wu", [I_FF, H], F32, kind="ExternalInput")
    wd_d = nc.dram_tensor("wd", [H, I_FF], F32, kind="ExternalInput")
    eye_d = nc.dram_tensor("eye_bf", [128, 128], BF16, kind="ExternalInput")
    mlo_d = nc.dram_tensor("mask_lo", [128, 128], F32, kind="ExternalInput")
    mhi_d = nc.dram_tensor("mask_hi", [128, 128], F32, kind="ExternalInput")
    kones_d = nc.dram_tensor("kones", [128, NET], BF16, kind="ExternalInput")
    ones1_d = nc.dram_tensor("ones1", [1, 128], BF16, kind="ExternalInput")
    out_d = nc.dram_tensor("out", [T, H], F32, kind="ExternalOutput")

    qwg_d = nc.dram_tensor("qwg", [I_FF, H], BF16)
    qwu_d = nc.dram_tensor("qwu", [I_FF, H], BF16)
    qwd_d = nc.dram_tensor("qwd", [H, I_FF], BF16)
    xres_d = nc.dram_tensor("xres", [T, H], F32,
                            kind="ExternalOutput" if debug else "Internal")
    dbg = {}
    if debug:
        dbg["xq0"] = nc.dram_tensor("dbg_xq0", [128, H], BF16, kind="ExternalOutput")
        dbg["qwg0"] = nc.dram_tensor("dbg_qwg0", [128, H], BF16, kind="ExternalOutput")
        dbg["xlt0"] = nc.dram_tensor("dbg_xlt0", [128, 1024], BF16, kind="ExternalOutput")
        dbg["kfm0"] = nc.dram_tensor("dbg_kfm0", [128, 2048], BF16, kind="ExternalOutput")
        dbg["qtm0"] = nc.dram_tensor("dbg_qtm0", [128, 2048], BF16, kind="ExternalOutput")
        dbg["pnb0"] = nc.dram_tensor("dbg_pnb0", [128, 1152], BF16, kind="ExternalOutput")
        dbg["afm0"] = nc.dram_tensor("dbg_afm0", [128, 2048], BF16, kind="ExternalOutput")
        dbg["rbc0"] = nc.dram_tensor("dbg_rbc0", [128, 128], BF16, kind="ExternalOutput")
        dbg["misc0"] = nc.dram_tensor("dbg_misc0", [128, 512], F32, kind="ExternalOutput")
        dbg["vtm0"] = nc.dram_tensor("dbg_vtm0", [128, 8192], BF16, kind="ExternalOutput")

    def sb(name, shape, dt):
        return es.enter_context(nc.sbuf_tensor(name, shape, dt))

    QF = 1024
    sAbs = sb("sAbs", [128, QF], F32)
    sAy = sb("sAy", [128, QF], F32)
    sC1 = sb("sC1", [128, QF], F32)
    sC2 = sb("sC2", [128, QF], F32)
    sR = sb("sR", [128, QF], F32)
    sT = sb("sT", [128, QF], F32)
    sSgn = sb("sSgn", [128, QF], BF16)
    RSx = sb("RSx", [128, QF], F32)
    Sxx = sb("Sxx", [128, QF], F32)
    amax = sb("amax", [128, 128], F32)
    ebi = sb("ebi", [128, 128], I32)
    bR = sb("bR", [128, 128], I32)
    bS = sb("bS", [128, 128], I32)
    eFs = sb("eFs", [128, 128], F32)
    rFs = sb("rFs", [128, 128], F32)
    lA = sb("lA", [128, 128], F32)
    lC = sb("lC", [128, 128], F32)
    lF = sb("lF", [128, 128], F32)
    SxB = sb("SxB", [128, 128], F32)
    RSxB = sb("RSxB", [128, 128], F32)
    LNSQ = sb("LNSQ", [128, 1024], F32)
    sm = [sb(f"sm{i}", [128, 1], F32) for i in range(10)]
    eye = sb("eye", [128, 128], BF16)
    mlo = sb("mlo", [128, 128], F32)
    mhi = sb("mhi", [128, 128], F32)
    kon = sb("kon", [128, NET], BF16)
    on1 = sb("on1", [1, 128], BF16)
    ptr = [es.enter_context(nc.psum_tensor(f"ptr{i}", [128, 1024], BF16))
           for i in range(2)]

    prog = Prog()
    s_ld = Sem(es.enter_context(nc.semaphore("s_ld")))
    s_gl = Sem(es.enter_context(nc.semaphore("s_gl")))
    s_pe = Sem(es.enter_context(nc.semaphore("s_pe")))
    s_ve = Sem(es.enter_context(nc.semaphore("s_ve")))
    s_ac = Sem(es.enter_context(nc.semaphore("s_ac")))
    s_st = Sem(es.enter_context(nc.semaphore("s_st")))
    s_l0 = Sem(es.enter_context(nc.semaphore("s_l0")))
    s_l1 = Sem(es.enter_context(nc.semaphore("s_l1")))
    s_g0 = Sem(es.enter_context(nc.semaphore("s_g0")))
    s_g1 = Sem(es.enter_context(nc.semaphore("s_g1")))
    ALL_SEMS = [s_ld, s_gl, s_pe, s_ve, s_ac, s_st, s_l0, s_l1, s_g0, s_g1]

    def fence(engines=ENGINES):
        for e in engines:
            for s in ALL_SEMS:
                prog.wait(e, s, s.n)

    prog.dma("sync", eye[:, :], eye_d[:, :], s_ld)
    prog.dma("sync", mlo[:, :], mlo_d[:, :], s_ld)
    prog.dma("sync", mhi[:, :], mhi_d[:, :], s_ld)
    prog.dma("sync", kon[:, :], kones_d[:, :], s_ld)
    prog.dma("sync", on1[:, :], ones1_d[:, :], s_ld)
    CONST_LD = s_ld.n

    guard = {"q_ve": 0, "q_ac": 0, "ln_ve": 0, "ln_ac": 0}

    def emit_quant(src_ap, dst_ap, Fc, is_weight, src_ready=(), g_off=0):
        G = Fc // 32
        imq = 0.25 if is_weight else 1.0
        clip = 6.0 if is_weight else 7.5
        prog.wait("scalar", s_ve, guard["q_ve"])
        for s, v in src_ready:
            prog.wait("scalar", s, v)
        prog.op("scalar", lambda E: E.activation(sAbs[:, :Fc], src_ap, AF.Abs))
        a_sgn = prog.op("scalar", lambda E: E.activation(
            sSgn[:, :Fc], src_ap, AF.Sign), s_ac)
        for s, v in src_ready:
            prog.wait("vector", s, v)
        v_red = None
        if is_weight:
            v_red = prog.op("vector", lambda E: E.tensor_reduce(
                amax[:, :G], src_ap.rearrange("p (g k) -> p g k", k=32),
                AX.X, ALU.max, apply_absolute_value=True), s_ve)
            prog.seq("vector", lambda E: E.tensor_scalar(
                ebi[:, :G], amax[:, :G].bitcast(I32), 23, None,
                ALU.logical_shift_right), s_ve)
            prog.seq("vector", lambda E: E.tensor_copy(eFs[:, :G], ebi[:, :G]), s_ve)
            prog.seq("vector", lambda E: E.tensor_scalar(
                eFs[:, :G], eFs[:, :G], 2.0, 95.0, ALU.subtract, ALU.max), s_ve)
            prog.seq("vector", lambda E: E.tensor_scalar(
                eFs[:, :G], eFs[:, :G], 159.0, None, ALU.min), s_ve)
            prog.seq("vector", lambda E: E.tensor_scalar(
                rFs[:, :G], eFs[:, :G], -1.0, 254.0, ALU.mult, ALU.add), s_ve)
            prog.seq("vector", lambda E: E.tensor_copy(bS[:, :G], eFs[:, :G]), s_ve)
            prog.seq("vector", lambda E: E.tensor_copy(bR[:, :G], rFs[:, :G]), s_ve)
            prog.seq("vector", lambda E: E.tensor_scalar(
                bS[:, :G], bS[:, :G], 23, None, ALU.logical_shift_left), s_ve)
            prog.seq("vector", lambda E: E.tensor_scalar(
                bR[:, :G], bR[:, :G], 23, None, ALU.logical_shift_left), s_ve)
            rsx_b = bR[:, :G].bitcast(F32).unsqueeze(2).broadcast_to([128, G, 32])
            sxx_b = bS[:, :G].bitcast(F32).unsqueeze(2).broadcast_to([128, G, 32])
        else:
            rsx_b = RSxB[:, g_off:g_off + G].unsqueeze(2).broadcast_to([128, G, 32])
            sxx_b = SxB[:, g_off:g_off + G].unsqueeze(2).broadcast_to([128, G, 32])
        prog.wait("vector", s_ve, s_ve.n)
        prog.wait("vector", s_ac, a_sgn)
        prog.op("vector", lambda E: E.tensor_tensor(
            sAy[:, :Fc].rearrange("p (g k) -> p g k", k=32),
            sAbs[:, :Fc].rearrange("p (g k) -> p g k", k=32),
            rsx_b, ALU.mult))
        prog.op("vector", lambda E: E.tensor_scalar(
            sC1[:, :Fc], sAy[:, :Fc], 2.0, None, ALU.is_ge))
        prog.op("vector", lambda E: E.tensor_scalar(
            sC2[:, :Fc], sAy[:, :Fc], 4.0, None, ALU.is_ge))
        prog.op("vector", lambda E: E.tensor_scalar(
            sC1[:, :Fc], sC1[:, :Fc], -imq, 2.0 * imq, ALU.mult, ALU.add))
        prog.op("vector", lambda E: E.tensor_scalar(
            sC2[:, :Fc], sC2[:, :Fc], -2.0, 4.0, ALU.mult, ALU.add))
        prog.op("vector", lambda E: E.tensor_tensor(
            sR[:, :Fc], sC1[:, :Fc], sC2[:, :Fc], ALU.mult))
        prog.op("vector", lambda E: E.tensor_tensor(
            sT[:, :Fc], sAy[:, :Fc], sR[:, :Fc], ALU.mult))
        if not is_weight:
            prog.op("vector", lambda E: E.tensor_scalar(
                sC1[:, :Fc], sT[:, :Fc], 8.0, None, ALU.is_ge))
            prog.op("vector", lambda E: E.scalar_tensor_tensor(
                sC2[:, :Fc], sC1[:, :Fc], -8.0, sT[:, :Fc], ALU.mult, ALU.add))
            prog.op("vector", lambda E: E.tensor_scalar(
                sC1[:, :Fc], sC2[:, :Fc], 4.0, None, ALU.is_ge))
            prog.op("vector", lambda E: E.scalar_tensor_tensor(
                sAbs[:, :Fc], sC1[:, :Fc], -4.0, sC2[:, :Fc], ALU.mult, ALU.add))
            pl = sAbs
        else:
            pl = sT
        prog.op("vector", lambda E: E.tensor_scalar(
            sC1[:, :Fc], pl[:, :Fc], 2.0, None, ALU.is_ge))
        prog.op("vector", lambda E: E.scalar_tensor_tensor(
            sC2[:, :Fc], sC1[:, :Fc], -2.0, pl[:, :Fc], ALU.mult, ALU.add))
        prog.op("vector", lambda E: E.tensor_scalar(
            sC1[:, :Fc], sC2[:, :Fc], 0.5, None, ALU.is_gt))
        prog.op("vector", lambda E: E.tensor_scalar(
            sAbs[:, :Fc], sC2[:, :Fc], 1.5, None, ALU.is_ge))
        prog.op("vector", lambda E: E.tensor_tensor(
            sC1[:, :Fc], sC1[:, :Fc], sAbs[:, :Fc], ALU.add))
        prog.op("vector", lambda E: E.tensor_tensor(
            sC2[:, :Fc], sC2[:, :Fc], sC1[:, :Fc], ALU.subtract))
        prog.op("vector", lambda E: E.tensor_tensor(
            sT[:, :Fc], sT[:, :Fc], sC2[:, :Fc], ALU.subtract))
        prog.op("vector", lambda E: E.reciprocal(sR[:, :Fc], sR[:, :Fc]))
        prog.op("vector", lambda E: E.tensor_tensor(
            sAbs[:, :Fc], sT[:, :Fc], sR[:, :Fc], ALU.mult))
        prog.op("vector", lambda E: E.tensor_scalar(
            sAbs[:, :Fc], sAbs[:, :Fc], clip, None, ALU.min))
        prog.op("vector", lambda E: E.tensor_tensor(
            sAbs[:, :Fc].rearrange("p (g k) -> p g k", k=32),
            sAbs[:, :Fc].rearrange("p (g k) -> p g k", k=32),
            sxx_b, ALU.mult))
        v = prog.op("vector", lambda E: E.tensor_tensor(
            dst_ap, sAbs[:, :Fc], sSgn[:, :Fc], ALU.mult), s_ve)
        guard["q_ve"] = v
        return {"done": (s_ve, v),
                "src_free": [(s_ac, a_sgn), (s_ve, v if v_red is None else v_red)]}

    def emit_act_amax(src_ap, Fc, g_off, src_ready=()):
        G = Fc // 32
        for s, v in src_ready:
            prog.wait("vector", s, v)
        return prog.op("vector", lambda E: E.tensor_reduce(
            amax[:, g_off:g_off + G],
            src_ap.rearrange("p (g k) -> p g k", k=32),
            AX.X, ALU.max, apply_absolute_value=True), s_ve)

    def emit_act_scales(Gtot):
        g = slice(0, Gtot)
        prog.seq("vector", lambda E: E.tensor_scalar(
            lA[:, g], amax[:, g], 2.0 ** -32, 2.0 ** 30, ALU.max, ALU.min), s_ve)
        prog.seq("vector", lambda E: E.tensor_scalar(
            lA[:, g], lA[:, g], 2.0 ** 32, None, ALU.mult), s_ve)
        prog.seq("vector", lambda E: E.tensor_scalar(
            lC[:, g], lA[:, g], 2.0 ** 32, None, ALU.is_ge), s_ve)
        prog.seq("vector", lambda E: E.tensor_scalar(
            lF[:, g], lC[:, g], -1.0, 1.0, ALU.mult, ALU.add), s_ve)
        prog.seq("vector", lambda E: E.tensor_scalar(
            SxB[:, g], lC[:, g], 2.0 ** 32, None, ALU.mult), s_ve)
        prog.seq("vector", lambda E: E.tensor_tensor(
            SxB[:, g], SxB[:, g], lF[:, g], ALU.add), s_ve)
        prog.seq("vector", lambda E: E.tensor_scalar(
            rFs[:, g], lC[:, g], 2.0 ** -32, None, ALU.mult), s_ve)
        prog.seq("vector", lambda E: E.tensor_tensor(
            rFs[:, g], rFs[:, g], lF[:, g], ALU.add), s_ve)
        prog.seq("vector", lambda E: E.tensor_tensor(
            lA[:, g], lA[:, g], rFs[:, g], ALU.mult), s_ve)
        for K in (16, 8, 4, 2, 1):
            prog.seq("vector", lambda E, kk=K: E.tensor_scalar(
                lC[:, g], lA[:, g], float(2.0 ** kk), None, ALU.is_ge), s_ve)
            prog.seq("vector", lambda E, kk=K: E.tensor_scalar(
                lF[:, g], lC[:, g], float(2.0 ** -kk - 1.0), 1.0,
                ALU.mult, ALU.add), s_ve)
            prog.seq("vector", lambda E: E.tensor_tensor(
                lA[:, g], lA[:, g], lF[:, g], ALU.mult), s_ve)
            prog.seq("vector", lambda E, kk=K: E.tensor_scalar(
                lF[:, g], lC[:, g], float(2.0 ** kk - 1.0), 1.0,
                ALU.mult, ALU.add), s_ve)
            prog.seq("vector", lambda E: E.tensor_tensor(
                SxB[:, g], SxB[:, g], lF[:, g], ALU.mult), s_ve)
        prog.seq("vector", lambda E: E.tensor_scalar(
            SxB[:, g], SxB[:, g], 2.0 ** -34, None, ALU.mult), s_ve)
        return prog.seq("vector", lambda E: E.reciprocal(
            RSxB[:, g], SxB[:, g]), s_ve)

    def emit_ln(src_ap, dst_ap, src_ready=()):
        prog.wait("scalar", s_ve, guard["ln_ve"])
        for s, v in src_ready:
            prog.wait("scalar", s, v)
        prog.op("scalar", lambda E: E.activation(
            LNSQ[:, :], src_ap[:, 0:1024], AF.Square, accum_out=sm[1][:, :]))
        a_sq = prog.op("scalar", lambda E: E.activation(
            LNSQ[:, :], src_ap[:, 1024:2048], AF.Square, accum_out=sm[8][:, :]),
            s_ac)
        for s, v in src_ready:
            prog.wait("vector", s, v)
        prog.seq("vector", lambda E: E.tensor_reduce(
            sm[0][:, :], src_ap, AX.X, ALU.add), s_ve)
        prog.seq("vector", lambda E: E.tensor_scalar(
            sm[2][:, :], sm[0][:, :], 1.0 / H, None, ALU.mult), s_ve)
        prog.wait("vector", s_ac, a_sq)
        prog.seq("vector", lambda E: E.tensor_tensor(
            sm[3][:, :], sm[1][:, :], sm[8][:, :], ALU.add), s_ve)
        prog.seq("vector", lambda E: E.tensor_scalar(
            sm[3][:, :], sm[3][:, :], 1.0 / H, None, ALU.mult), s_ve)
        prog.seq("vector", lambda E: E.tensor_tensor(
            sm[4][:, :], sm[2][:, :], sm[2][:, :], ALU.mult), s_ve)
        prog.seq("vector", lambda E: E.tensor_tensor(
            sm[4][:, :], sm[3][:, :], sm[4][:, :], ALU.subtract), s_ve)
        v0b = prog.seq("vector", lambda E: E.tensor_scalar(
            sm[4][:, :], sm[4][:, :], 1e-5, None, ALU.add), s_ve)
        prog.wait("scalar", s_ve, v0b)
        a1 = prog.op("scalar", lambda E: E.activation(
            sm[5][:, :], sm[4][:, :], AF.Sqrt), s_ac)
        prog.wait("vector", s_ac, a1)
        v1 = prog.seq("vector", lambda E: E.reciprocal(
            sm[6][:, :], sm[5][:, :]), s_ve)
        prog.wait("vector", s_ve, v1)
        v2 = prog.op("vector", lambda E: E.tensor_scalar(
            dst_ap, src_ap, sm[2][:, :], sm[6][:, :], ALU.subtract, ALU.mult),
            s_ve)
        guard["ln_ve"] = v2
        return {"done": (s_ve, v2),
                "src_free": [(s_ve, v2), (s_ac, a_sq)]}

    tr_state = {"i": 0, "done": [None, None]}

    def emit_transpose(in_ap, out_ap, wait_in=(), ev_wait=()):
        i = tr_state["i"] % 2
        tr_state["i"] += 1
        if tr_state["done"][i] is not None:
            prog.wait("tensor", s_ve, tr_state["done"][i])
        for s, v in wait_in:
            prog.wait("tensor", s, v)
        p = prog.op("tensor", lambda E, a=in_ap, o=ptr[i]: E.transpose(
            o[:, 0:128], a, eye[:, :]), s_pe)
        for s, v in ev_wait:
            prog.wait("vector", s, v)
        prog.wait("vector", s_pe, p)
        v = prog.op("vector", lambda E, o=out_ap, pt=ptr[i]:
                    E.tensor_copy(o, pt[:, 0:128]), s_ve)
        tr_state["done"][i] = v
        return v

    with (nc.sbuf_tensor("wblk", [128, 2 * I_FF], F32) as wblk,
          nc.sbuf_tensor("qblk", [128, 2 * I_FF], BF16) as qblk):
        last_rd = {}
        last_st = {}
        jobs = []
        for m_d, q_d, nblk, K in ((wg_d, qwg_d, I_FF // 128, H),
                                  (wu_d, qwu_d, I_FF // 128, H),
                                  (wd_d, qwd_d, H // 128, I_FF)):
            for blk in range(nblk):
                jobs.append((m_d, q_d, blk, K))
        ldsem = [s_l0, s_l1]
        stsem = [s_g0, s_g1]
        for idx, (m_d, q_d, blk, K) in enumerate(jobs):
            b = idx % 2
            for s, v in last_rd.get(b, ()):
                prog.wait("sync", s, v)
            ld = prog.dma("sync", wblk[:, b * I_FF:b * I_FF + K],
                          m_d[blk * 128:(blk + 1) * 128, :], ldsem[b])
            frees = []
            mil = None
            for ch in range(K // QF):
                o = b * I_FF + ch * QF
                r = emit_quant(wblk[:, o:o + QF], qblk[:, o:o + QF],
                               QF, True, src_ready=[(ldsem[b], ld)])
                frees.extend(r["src_free"])
                mil = r["done"]
            last_rd[b] = frees
            if b in last_st:
                prog.wait("gpsimd", stsem[b], last_st[b])
            prog.wait("gpsimd", *mil)
            last_st[b] = prog.dma("gpsimd", q_d[blk * 128:(blk + 1) * 128, :],
                                  qblk[:, b * I_FF:b * I_FF + K], stsem[b])
        P0_ST = (s_g0.n, s_g1.n)
    if debug:
        prog.wait("gpsimd", s_g0, P0_ST[0])
        prog.wait("gpsimd", s_g1, P0_ST[1])
        prog.dma("gpsimd", dbg["qwg0"][:, :], qwg_d[0:128, :], s_gl)

    fence()

    NB = 6
    pab = ExitStack()
    KFM = pab.enter_context(nc.sbuf_tensor("KFM", [128, KVH * 2048], BF16))
    VTM = pab.enter_context(nc.sbuf_tensor("VTM", [128, NET * 512], BF16))
    QTM = pab.enter_context(nc.sbuf_tensor("QTM", [128, NT * 2048], BF16))
    pa = ExitStack()
    XLT = pa.enter_context(nc.sbuf_tensor("XLT", [128, 16 * 1024], BF16))
    WKB = pa.enter_context(nc.sbuf_tensor("WKB", [128, 16 * 512], BF16))
    WVB = pa.enter_context(nc.sbuf_tensor("WVB", [128, 16 * 512], BF16))
    XT = pa.enter_context(nc.sbuf_tensor("XT", [128, H], F32))
    XLN = pa.enter_context(nc.sbuf_tensor("XLN", [128, H], BF16))
    KTMP = pa.enter_context(nc.sbuf_tensor("KTMP", [128, 512], BF16))
    pp = [pa.enter_context(nc.psum_tensor(f"pp{i}", [128, 512], F32))
          for i in range(NB)]

    for hc in range(16):
        prog.dma("gpsimd", WKB[:, hc * 512:(hc + 1) * 512],
                 wk_d[hc * 128:(hc + 1) * 128, :], s_gl)
        prog.dma("gpsimd", WVB[:, hc * 512:(hc + 1) * 512],
                 wv_d[hc * 128:(hc + 1) * 128, :], s_gl)
    WKV_LD = s_gl.n
    prog.wait("tensor", s_ld, CONST_LD)

    xt_free = {}
    pp_guard = {"v": None}

    def proj(set_i, out_write, rhs_of_hc):
        for b0 in range(0, 8, NB):
            bw = min(NB, 8 - b0)
            if pp_guard["v"] is not None:
                prog.wait("tensor", s_ve, pp_guard["v"])
            last_mm = None
            for hc in range(16):
                for tci in range(bw):
                    tc = b0 + tci
                    lhs = XLT[:, hc * 1024 + tc * 128: hc * 1024 + (tc + 1) * 128]
                    rhs = rhs_of_hc(hc)
                    last_mm = prog.op(
                        "tensor",
                        lambda E, p=pp[tci], l=lhs, r=rhs,
                               st0=(hc == 0), sp=(hc == 15):
                            E.matmul(p[:, :], l, r, start=st0, stop=sp),
                        s_pe if hc == 15 else None)
            for tci in range(bw):
                prog.wait("vector", s_pe, last_mm)
                out_write(b0 + tci, pp[tci])
            pp_guard["v"] = s_ve.n

    for st in range(2):
        for j in range(8):
            et = st * 8 + j
            prog.wait("vector", s_pe, s_pe.n)
            for s, v in xt_free.get(0, ()):
                prog.wait("sync", s, v)
            ld = prog.dma("sync", XT[:, :],
                          x_ext[et * 128:(et + 1) * 128, :], s_ld)
            r = emit_ln(XT[:, :], XLN[:, :], src_ready=[(s_ld, ld)])
            xt_free[0] = r["src_free"]
            for hc in range(16):
                emit_transpose(
                    XLN[:, hc * 128:(hc + 1) * 128],
                    XLT[:, hc * 1024 + j * 128: hc * 1024 + (j + 1) * 128],
                    wait_in=[r["done"]])
        prog.wait("tensor", s_gl, WKV_LD)
        prog.wait("tensor", s_ve, s_ve.n)

        def k_write(tc, ps):
            et = st * 8 + tc
            prog.wait("vector", s_pe, s_pe.n)
            v = prog.op("vector", lambda E, p=ps: E.tensor_copy(
                KTMP[:, :], p[:, :]), s_ve)
            for g in range(KVH):
                emit_transpose(
                    KTMP[:, g * 128:(g + 1) * 128],
                    KFM[:, g * 2048 + et * 128: g * 2048 + (et + 1) * 128],
                    wait_in=[(s_ve, v)])
        proj(st, k_write, lambda hc: WKB[:, hc * 512:(hc + 1) * 512])

        def v_write(tc, ps):
            et = st * 8 + tc
            prog.op("vector", lambda E, p=ps, e=et: E.tensor_copy(
                VTM[:, e * 512:(e + 1) * 512], p[:, :]), s_ve)
        proj(st, v_write, lambda hc: WVB[:, hc * 512:(hc + 1) * 512])

        if st == 1:
            for o5 in range(4):
                prog.wait("gpsimd", s_pe, s_pe.n)
                gl = None
                for hc in range(16):
                    gl = prog.dma("gpsimd", WVB[:, hc * 512:(hc + 1) * 512],
                                  wq_d[hc * 128:(hc + 1) * 128,
                                       o5 * 512:(o5 + 1) * 512], s_gl)
                prog.wait("tensor", s_gl, gl)

                def q_write(tc, ps, o5=o5):
                    prog.op("vector", lambda E, p=ps, t=tc: E.tensor_scalar(
                        QTM[:, t * 2048 + o5 * 512: t * 2048 + (o5 + 1) * 512],
                        p[:, :], INV_SQRT_D, None, ALU.mult), s_ve)
                proj(st, q_write,
                     lambda hc: WVB[:, hc * 512:(hc + 1) * 512])
    if debug:
        prog.wait("gpsimd", s_ve, s_ve.n)
        prog.dma("gpsimd", dbg["xlt0"][:, :], XLT[:, 0:1024], s_st)
        prog.dma("gpsimd", dbg["kfm0"][:, :], KFM[:, 0:2048], s_st)
        prog.dma("gpsimd", dbg["qtm0"][:, :], QTM[:, 0:2048], s_st)
        prog.wait("gpsimd", s_st, s_st.n)
    fence()
    pa.close()

    pb = ExitStack()
    WOB = pb.enter_context(nc.sbuf_tensor("WOB", [128, 16 * 2048], BF16))
    PNBb = pb.enter_context(nc.sbuf_tensor("PNBb", [128, WIN], BF16))
    PTB = pb.enter_context(nc.sbuf_tensor("PTB", [128, 9 * 128], BF16))
    QFM = pb.enter_context(nc.sbuf_tensor("QFM", [128, 16 * 128], BF16))
    AFM = pb.enter_context(nc.sbuf_tensor("AFM", [128, 16 * 128], BF16))
    XRT = pb.enter_context(nc.sbuf_tensor("XRT", [128, H], F32))
    AFMF = pb.enter_context(nc.sbuf_tensor("AFMF", [128, 16 * 128], BF16))
    ps_s = pb.enter_context(nc.psum_tensor("ps_s", [128, 1536], F32))
    ps_wo = pb.enter_context(nc.psum_tensor("ps_wo", [128, 512], F32))
    ps_av = pb.enter_context(nc.psum_tensor("ps_av", [128, 512], F32))
    ps_q1 = ps_av[:, 128:129]

    for fc in range(16):
        prog.dma("gpsimd", WOB[:, fc * 2048:(fc + 1) * 2048],
                 wo_d[fc * 128:(fc + 1) * 128, :], s_gl)
    WO_LD = s_gl.n

    xst = 0
    exp_prev = 0
    tr_prev = 0
    for qt in range(NT):
        prog.wait("vector", s_pe, s_pe.n)
        prog.wait("scalar", s_pe, s_pe.n)
        for h in range(NH):
            emit_transpose(
                QTM[:, qt * 2048 + h * 128: qt * 2048 + (h + 1) * 128],
                QFM[:, h * 128:(h + 1) * 128])
        qfm_ready = s_ve.n
        if xst:
            prog.wait("sync", s_st, xst)
        xld = prog.dma("sync", XRT[:, :],
                       x_ext[(8 + qt) * 128:(9 + qt) * 128, :], s_ld)
        for h in range(NH):
            g = h // 4
            prog.wait("tensor", s_ve, qfm_ready)
            prog.wait("tensor", s_ac, exp_prev)
            kbase = qt * 128
            mm3 = None
            for c0, cw in ((0, 512), (512, 512), (1024, 128)):
                mm3 = prog.op("tensor", lambda E, c=c0, w=cw, hh=h, gg=g, kb=kbase:
                              E.matmul(ps_s[:, c:c + w],
                                       QFM[:, hh * 128:(hh + 1) * 128],
                                       KFM[:, gg * 2048 + kb + c:
                                           gg * 2048 + kb + c + w],
                                       start=True, stop=True),
                              s_pe if c0 == 1024 else None)
            prog.wait("vector", s_pe, mm3)
            prog.wait("vector", s_ld, CONST_LD)
            prog.op("vector", lambda E: E.tensor_tensor(
                ps_s[:, 0:128], ps_s[:, 0:128], mlo[:, :], ALU.add))
            prog.op("vector", lambda E: E.tensor_tensor(
                ps_s[:, 1024:1152], ps_s[:, 1024:1152], mhi[:, :], ALU.add))
            prog.seq("vector", lambda E: E.tensor_reduce(
                sm[0][:, :], ps_s[:, 0:WIN], AX.X, ALU.max), s_ve)
            vmax = prog.seq("vector", lambda E: E.tensor_scalar(
                sm[1][:, :], sm[0][:, :], -1.0, None, ALU.mult), s_ve)
            prog.wait("scalar", s_ve, vmax)
            if debug and qt == 0 and h == 1:
                prog.wait("scalar", s_st, dbg["_pnb_done"])
            prog.wait("scalar", s_pe, tr_prev)
            aex = 0
            for c0, cw in ((0, 512), (512, 512), (1024, 128)):
                aex = prog.op(
                    "scalar", lambda E, c=c0, w=cw:
                        E.activation(PNBb[:, c:c + w], ps_s[:, c:c + w],
                                     AF.Exp, bias=sm[1][:, :]), s_ac)
            exp_prev = aex
            av_prev_val = s_pe.n
            prog.wait("tensor", s_ac, aex)
            for kc in range(9):
                emit_transpose(PNBb[:, kc * 128:(kc + 1) * 128],
                               PTB[:, kc * 128:(kc + 1) * 128],
                               ev_wait=[(s_pe, av_prev_val)])
            tr_prev = s_pe.n
            pt_ready = s_ve.n
            prog.wait("tensor", s_ve, pt_ready)
            smm = None
            avm = None
            for kc in range(9):
                et = qt + kc
                smm = prog.op("tensor", lambda E, e=et, k=kc:
                              E.matmul(ps_q1[:, :],
                                       PTB[:, k * 128:(k + 1) * 128],
                                       kon[:, e:e + 1],
                                       start=(k == 0), stop=(k == 8)),
                              s_pe if kc == 8 else None)
            for kc in range(9):
                et = qt + kc
                avm = prog.op("tensor", lambda E, e=et, k=kc, gg=g:
                              E.matmul(ps_av[:, 0:128],
                                       PTB[:, k * 128:(k + 1) * 128],
                                       VTM[:, e * 512 + gg * 128:
                                           e * 512 + (gg + 1) * 128],
                                       start=(k == 0), stop=(k == 8)),
                              s_pe if kc == 8 else None)
            prog.wait("vector", s_pe, avm)
            vrp = prog.seq("vector", lambda E: E.reciprocal(
                sm[9][:, :], ps_q1[:, :]), s_ve)
            prog.wait("vector", s_ve, vrp)
            prog.op("vector", lambda E, hh=h: E.tensor_scalar(
                AFM[:, hh * 128:(hh + 1) * 128], ps_av[:, 0:128],
                sm[9][:, :], None, ALU.mult), s_ve)
            if debug and qt == 0 and h == 0:
                prog.seq("vector", lambda E: E.tensor_copy(
                    sAy[:, 0:1], ps_q1[:, :]), s_ve)
                prog.seq("vector", lambda E: E.tensor_copy(
                    sAy[:, 384:400], kon[:, :]), s_ve)
                vd = prog.seq("vector", lambda E: E.tensor_copy(
                    sAy[0:1, 400:528], on1[0:1, :]), s_ve)
                prog.wait("gpsimd", s_ve, vd)
                prog.dma("gpsimd", dbg["misc0"][:, :], sAy[:, 0:512], s_st)
                prog.dma("gpsimd", dbg["pnb0"][:, :], PNBb[:, :], s_st)
                prog.dma("gpsimd", dbg["vtm0"][:, :], VTM[:, :], s_st)
                dbg["_pnb_done"] = s_st.n
        if debug and qt == 0:
            prog.wait("gpsimd", s_ve, s_ve.n)
            prog.dma("gpsimd", dbg["afm0"][:, :], AFM[:, :], s_st)
        afm_ready = s_ve.n
        for fc in range(16):
            emit_transpose(AFM[:, fc * 128:(fc + 1) * 128],
                           AFMF[:, fc * 128:(fc + 1) * 128],
                           wait_in=[(s_ve, afm_ready)])
        afmf_ready = s_ve.n
        prog.wait("tensor", s_gl, WO_LD)
        prog.wait("tensor", s_ve, afmf_ready)
        prog.wait("vector", s_ld, xld)
        wo_evac = None
        for o5 in range(4):
            wom = None
            for fc in range(16):
                wom = prog.op("tensor", lambda E, f=fc, o=o5:
                              E.matmul(ps_wo[:, :],
                                       AFMF[:, f * 128:(f + 1) * 128],
                                       WOB[:, f * 2048 + o * 512:
                                           f * 2048 + (o + 1) * 512],
                                       start=(f == 0), stop=(f == 15)),
                              s_pe if fc == 15 else None)
            prog.wait("vector", s_pe, wom)
            wo_evac = prog.op("vector", lambda E, o=o5: E.tensor_tensor(
                XRT[:, o * 512:(o + 1) * 512], ps_wo[:, :],
                XRT[:, o * 512:(o + 1) * 512], ALU.add), s_ve)
            if o5 < 3:
                prog.wait("tensor", s_ve, wo_evac)
        prog.wait("sync", s_ve, wo_evac)
        xst = prog.dma("sync", xres_d[qt * 128:(qt + 1) * 128, :],
                       XRT[:, :], s_st)
    PB_XST = s_st.n
    fence()
    pb.close()
    pab.close()

    pf = ExitStack()
    XRES = pf.enter_context(nc.sbuf_tensor("XRES", [128, NT * H], F32))
    XQT = pf.enter_context(nc.sbuf_tensor("XQT", [128, 16 * 1024], BF16))
    QIT = pf.enter_context(nc.sbuf_tensor("QIT", [128, 8 * 1024], BF16))
    WGT = pf.enter_context(nc.sbuf_tensor("WGT", [128, 16 * 512], BF16))
    WUT = pf.enter_context(nc.sbuf_tensor("WUT", [128, 16 * 512], BF16))

    XQB = pf.enter_context(nc.sbuf_tensor("XQB", [128, H], BF16))
    SGT = pf.enter_context(nc.sbuf_tensor("SGT", [128, 512], F32))
    ITC = pf.enter_context(nc.sbuf_tensor("ITC", [128, 2048], F32))
    XL2 = ITC
    QXC = pf.enter_context(nc.sbuf_tensor("QXC", [128, 1024], BF16))
    psG = pf.enter_context(nc.psum_tensor("psG", [128, 512], F32))
    psU = pf.enter_context(nc.psum_tensor("psU", [128, 512], F32))
    psD = pf.enter_context(nc.psum_tensor("psD", [128, 512], F32))

    prog.wait("sync", s_st, PB_XST)
    xres_ld = None
    for tc in range(NT):
        xres_ld = prog.dma("sync", XRES[:, tc * H:(tc + 1) * H],
                           xres_d[tc * 128:(tc + 1) * 128, :], s_ld)
    dbg_xq_st = 0
    for tc in range(NT):
        prog.wait("vector", s_pe, s_pe.n)
        prog.wait("vector", s_ac, s_ac.n)
        if dbg_xq_st:
            prog.wait("vector", s_st, dbg_xq_st)
        r = emit_ln(XRES[:, tc * H:(tc + 1) * H], XL2[:, :],
                    src_ready=[(s_ld, xres_ld)])
        emit_act_amax(XL2[:, 0:1024], 1024, 0, src_ready=[r["done"]])
        emit_act_amax(XL2[:, 1024:2048], 1024, 32)
        emit_act_scales(64)
        q1 = emit_quant(XL2[:, 0:1024], XQB[:, 0:1024], 1024, False,
                        src_ready=[r["done"]], g_off=0)
        q2 = emit_quant(XL2[:, 1024:2048], XQB[:, 1024:2048], 1024, False,
                        src_ready=[r["done"]], g_off=32)
        if debug and tc == 0:
            prog.wait("gpsimd", *q2["done"])
            prog.wait("gpsimd", *q1["done"])
            dbg_xq_st = prog.dma("gpsimd", dbg["xq0"][:, :], XQB[:, :], s_st)
        for hc in range(16):
            emit_transpose(
                XQB[:, hc * 128:(hc + 1) * 128],
                XQT[:, hc * 1024 + tc * 128: hc * 1024 + (tc + 1) * 128],
                wait_in=[q2["done"] if hc >= 8 else q1["done"]])
    xqt_ready = s_ve.n

    down_mil = None
    for islab in range(8):
        prog.wait("vector", s_pe, s_pe.n)
        for o5i in range(2):
            o5 = islab * 2 + o5i
            prog.wait("sync", s_pe, s_pe.n)
            gl = None
            for hc in range(16):
                prog.dma("sync", WGT[:, hc * 512:(hc + 1) * 512],
                         qwg_d[o5 * 512:(o5 + 1) * 512,
                               hc * 128:(hc + 1) * 128], s_ld, transpose=True)
                gl = prog.dma("sync", WUT[:, hc * 512:(hc + 1) * 512],
                              qwu_d[o5 * 512:(o5 + 1) * 512,
                                    hc * 128:(hc + 1) * 128], s_ld,
                              transpose=True)
            prog.wait("tensor", s_ld, gl)
            prog.wait("tensor", s_ve, xqt_ready)
            for half in range(2):
                itc_mil = None
                for tci in range(4):
                    tc = half * 4 + tci
                    prog.wait("vector", s_ac, s_ac.n)
                    prog.wait("scalar", s_ve, s_ve.n)
                    mmg = mmu = None
                    for hc in range(16):
                        lhs = XQT[:, hc * 1024 + tc * 128: hc * 1024 + (tc + 1) * 128]
                        mmg = prog.op("tensor", lambda E, l=lhs, h2=hc:
                                      E.matmul(psG[:, :], l,
                                               WGT[:, h2 * 512:(h2 + 1) * 512],
                                               start=(h2 == 0), stop=(h2 == 15)),
                                      s_pe if hc == 15 else None)
                    for hc in range(16):
                        lhs = XQT[:, hc * 1024 + tc * 128: hc * 1024 + (tc + 1) * 128]
                        mmu = prog.op("tensor", lambda E, l=lhs, h2=hc:
                                      E.matmul(psU[:, :], l,
                                               WUT[:, h2 * 512:(h2 + 1) * 512],
                                               start=(h2 == 0), stop=(h2 == 15)),
                                      s_pe if hc == 15 else None)
                    prog.wait("scalar", s_pe, mmg)
                    asl = prog.op("scalar", lambda E: E.activation(
                        SGT[:, :], psG[:, :], AF.Silu), s_ac)
                    prog.wait("vector", s_pe, mmu)
                    prog.wait("vector", s_ac, asl)
                    itc_mil = prog.op("vector", lambda E, t=tci: E.tensor_tensor(
                        ITC[:, t * 512:(t + 1) * 512], SGT[:, :], psU[:, :],
                        ALU.mult), s_ve)
                    prog.wait("tensor", s_ac, asl)
                    prog.wait("tensor", s_ve, itc_mil)
                emit_act_amax(ITC[:, :], 2048, 0, src_ready=[(s_ve, itc_mil)])
                emit_act_scales(64)
                for ch in range(2):
                    qm = emit_quant(ITC[:, ch * 1024:(ch + 1) * 1024], QXC[:, :],
                                    1024, False, g_off=ch * 32)
                    for j in range(8):
                        tcl = j // 4
                        jj = j % 4
                        ic = o5i * 4 + jj
                        tcg = half * 4 + ch * 2 + tcl
                        emit_transpose(
                            QXC[:, tcl * 512 + jj * 128:
                                tcl * 512 + (jj + 1) * 128],
                            QIT[:, ic * 1024 + tcg * 128:
                                ic * 1024 + (tcg + 1) * 128],
                            wait_in=[qm["done"]])
        qit_ready = s_ve.n
        for h5 in range(4):
            prog.wait("sync", s_pe, s_pe.n)
            dl = None
            for ici in range(8):
                ic = islab * 8 + ici
                dl = prog.dma("sync", WGT[:, ici * 512:(ici + 1) * 512],
                              qwd_d[h5 * 512:(h5 + 1) * 512,
                                    ic * 128:(ic + 1) * 128], s_ld,
                              transpose=True)
            prog.wait("tensor", s_ld, dl)
            prog.wait("tensor", s_ve, qit_ready)
            for tc in range(NT):
                mmd = None
                for ici in range(8):
                    lhs = QIT[:, ici * 1024 + tc * 128: ici * 1024 + (tc + 1) * 128]
                    mmd = prog.op("tensor", lambda E, l=lhs, i2=ici:
                                  E.matmul(psD[:, :], l,
                                           WGT[:, i2 * 512:(i2 + 1) * 512],
                                           start=(i2 == 0), stop=(i2 == 7)),
                                  s_pe if ici == 7 else None)
                prog.wait("vector", s_pe, mmd)
                down_mil = prog.op(
                    "vector", lambda E, t=tc, h2=h5: E.tensor_tensor(
                        XRES[:, t * H + h2 * 512: t * H + (h2 + 1) * 512],
                        psD[:, :],
                        XRES[:, t * H + h2 * 512: t * H + (h2 + 1) * 512],
                        ALU.add), s_ve)
                prog.wait("tensor", s_ve, down_mil)
    prog.wait("sync", s_ve, down_mil)
    for tc in range(NT):
        prog.dma("sync", out_d[tc * 128:(tc + 1) * 128, :],
                 XRES[:, tc * H:(tc + 1) * H], s_st)
    prog.wait("sync", s_st, s_st.n)
    pf.close()

    with nc.Block() as block:
        @block.sync
        def _(E):
            for fn in prog.steps["sync"]:
                fn(E)

        @block.vector
        def _(E):
            for fn in prog.steps["vector"]:
                fn(E)

        @block.scalar
        def _(E):
            for fn in prog.steps["scalar"]:
                fn(E)

        @block.tensor
        def _(E):
            for fn in prog.steps["tensor"]:
                fn(E)

        @block.gpsimd
        def _(E):
            for fn in prog.steps["gpsimd"]:
                fn(E)

    es.close()
    return nc


def make_inputs(full):
    x = np.asarray(full["L0_attn_hidden_states"], dtype=np.float32).reshape(8192, H)
    arrs = {
        "wq": np.ascontiguousarray(np.asarray(full["L0_attn_Wq"], dtype=np.float32)),
        "wk": np.ascontiguousarray(np.asarray(full["L0_attn_Wk"], dtype=np.float32)),
        "wv": np.ascontiguousarray(np.asarray(full["L0_attn_Wv"], dtype=np.float32)),
        "wo": np.ascontiguousarray(np.asarray(full["L0_attn_Wo"], dtype=np.float32)),
        "wg": np.ascontiguousarray(np.asarray(full["L0_ffn_w_gate"], dtype=np.float32)),
        "wu": np.ascontiguousarray(np.asarray(full["L0_ffn_w_up"], dtype=np.float32)),
        "wd": np.ascontiguousarray(np.asarray(full["L0_ffn_w_down"], dtype=np.float32)),
        "eye_bf": np.eye(128, dtype=ml_dtypes.bfloat16),
        "ones1": np.ones((1, 128), dtype=ml_dtypes.bfloat16),
    }
    i_ = np.arange(128)[:, None]
    j_ = np.arange(128)[None, :]
    arrs["mask_lo"] = np.where(j_ < i_, NEG, 0.0).astype(np.float32)
    arrs["mask_hi"] = np.where(j_ > i_, NEG, 0.0).astype(np.float32)
    in_maps = []
    for c in range(8):
        lo = 1024 * c - 1024
        xe = np.zeros((2048, H), dtype=np.float32)
        a = max(lo, 0)
        xe[a - lo:2048, :] = x[a:lo + 2048, :]
        kones = np.zeros((128, NET), dtype=ml_dtypes.bfloat16)
        for e in range(NET):
            for p in range(128):
                kones[p, e] = 1.0 if (1024 * c - 1024 + e * 128 + p) >= 0 else 0.0
        m = dict(arrs)
        m["x_ext"] = xe
        m["kones"] = kones
        in_maps.append(m)
    return in_maps


def kernel(**inputs):
    nc = build_program(debug=False)
    in_maps = make_inputs(inputs)
    res = run_bass_kernel_spmd(nc, in_maps, core_ids=list(range(8)))
    outs = [np.asarray(r["out"], dtype=np.float32) for r in res.results]
    return np.concatenate(outs, axis=0).reshape(1, 8192, H)
